# version 14
# baseline (speedup 1.0000x reference)
import numpy as np
import ml_dtypes
from contextlib import ExitStack

import concourse.bass as bass
import concourse.mybir as mybir
from concourse.bass_utils import run_bass_kernel_spmd

F32 = mybir.dt.float32
BF16 = mybir.dt.bfloat16
AF = mybir.ActivationFunctionType
ALU = mybir.AluOpType
AX = mybir.AxisListType

DIM = 1024
NKT = 8
WIN = 256
DI = 2730
NPAIR = 22
LN_EPS = 1e-5
QK_SCALE = 8.0
NEG = -1.0e30

ENGINES = ("sync", "tensor", "vector", "scalar", "gpsimd")


class Sem:
    __slots__ = ("h", "n", "name")

    def __init__(self, h, name):
        self.h, self.n, self.name = h, 0, name


class OpRef:
    __slots__ = ("i", "eng", "fn", "deps", "dma", "cost", "pos", "sem_val",
                 "finish", "nodep", "softs")

    def __init__(self, i, eng, fn, deps, dma, cost):
        self.i, self.eng, self.fn = i, eng, fn
        self.deps, self.dma, self.cost = deps, dma, cost
        self.pos = -1
        self.sem_val = None
        self.finish = 0.0
        self.nodep = True


def _default_cost(eng):
    return {"tensor": 250.0, "vector": 450.0, "scalar": 400.0,
            "gpsimd": 900.0, "sync": 300.0}[eng]


class KB:

    def __init__(self, nc, ctx):
        self.nc = nc
        self.ctx = ctx
        self.ops = []
        self.all_sems = []
        self.esem = {e: None for e in ENGINES}
        self.last_barrier = {e: None for e in ENGINES}
        self.last_op = {e: None for e in ENGINES}
        self.dma_last = {}
        self.dma_issues = {}
        self.compute_sems = set()

    def init_esems(self):
        for e in ENGINES:
            self.esem[e] = self.sem("es_" + e)

    def sem(self, name):
        s = Sem(self.ctx.enter_context(self.nc.semaphore(name)), name)
        self.all_sems.append(s)
        return s

    def op(self, eng, fn, waits=(), inc=None, by=16, cost=None):
        if inc is not None and inc in self.compute_sems:
            inc = None
        deps = []
        softs = []
        for tok in waits:
            if tok is None:
                continue
            if isinstance(tok, tuple) and len(tok) == 2 and tok[0] == "soft":
                if tok[1] is not None:
                    softs.append(tok[1])
                continue
            deps.append(tok)
            if not isinstance(tok, OpRef):
                sem_, val_ = tok
                lst = self.dma_issues.get(id(sem_))
                if lst is not None and val_ > 0:
                    import bisect
                    k_ = bisect.bisect_left(lst, (val_,), key=lambda x: (x[0],))
                    if k_ < len(lst):
                        softs.append(lst[k_][1])
        if self.last_barrier[eng] is not None:
            deps.append(self.last_barrier[eng])
        dma = None
        if inc is not None:
            inc.n += by
            dma = (inc, inc.n, by)
            if inc in self.dma_last:
                deps.append(self.dma_last[inc])
        o = OpRef(len(self.ops), eng, fn, deps, dma,
                  cost if cost is not None else _default_cost(eng))
        o.softs = softs
        for d in deps:
            if isinstance(d, OpRef):
                d.nodep = False
        self.ops.append(o)
        self.last_op[eng] = o
        if inc is not None:
            self.dma_last[inc] = o
            self.dma_issues.setdefault(id(inc), []).append((inc.n, o))
            return (inc, inc.n)
        return o

    def barrier(self):
        deps = [o for o in self.last_op.values() if o is not None]
        softs = []
        for s_ in self.all_sems:
            if s_.n > 0 and not s_.name.startswith("es_"):
                deps.append((s_, s_.n))
                lst = self.dma_issues.get(id(s_))
                if lst:
                    softs.append(lst[-1][1])
        for eng in ENGINES:
            b = OpRef(len(self.ops), eng, None, list(deps), None, 50.0)
            b.softs = list(softs)
            for d in deps:
                if isinstance(d, OpRef):
                    d.nodep = False
            self.ops.append(b)
            self.last_barrier[eng] = b
            self.last_op[eng] = b

    def emit(self):
        import heapq
        n = len(self.ops)
        ndep = [0] * n
        children = [[] for _ in range(n)]
        for o in self.ops:
            for d in o.deps:
                if isinstance(d, OpRef):
                    ndep[o.i] += 1
                    children[d.i].append(o.i)
            for d in o.softs:
                if isinstance(d, OpRef):
                    ndep[o.i] += 1
                    children[d.i].append(o.i)
        clock = {e: 0.0 for e in ENGINES}
        ready = {e: [] for e in ENGINES}
        est = [0.0] * n
        for o in self.ops:
            if ndep[o.i] == 0:
                heapq.heappush(ready[o.eng], (0.0, o.i))
        order = {e: [] for e in ENGINES}
        scheduled = 0
        pending = {e: [] for e in ENGINES}
        while scheduled < n:
            progressed = False
            for e in ENGINES:
                if not ready[e]:
                    continue
                er, i = heapq.heappop(ready[e])
                o = self.ops[i]
                start = max(clock[e], er)
                o.finish = start + o.cost
                clock[e] = o.finish
                o.pos = len(order[e])
                order[e].append(o)
                scheduled += 1
                progressed = True
                for c in children[i]:
                    ndep[c] -= 1
                    est[c] = max(est[c], o.finish)
                    if ndep[c] == 0:
                        heapq.heappush(ready[self.ops[c].eng], (est[c], c))
            if not progressed:
                raise RuntimeError("scheduler deadlock: cyclic deps")
        for e in ENGINES:
            cnt = 0
            for o in order[e]:
                if o.dma is None and not o.nodep:
                    cnt += 1
                    o.sem_val = cnt
            self.esem[e].n = cnt
        def make_thunk(o):
            wl = {}
            for d in o.deps:
                if isinstance(d, OpRef):
                    if d.dma is not None:
                        h, v = d.dma[0].h, d.dma[1]
                    else:
                        assert d.sem_val is not None, (d.i, d.eng, "no sem")
                        h, v = self.esem[d.eng].h, d.sem_val
                else:
                    h, v = d[0].h, d[1]
                    if v <= 0:
                        continue
                key = id(h)
                wl[key] = (h, max(wl.get(key, (None, 0))[1], v))
            wl = list(wl.values())
            inc_h = None
            inc_by = 0
            if o.dma is not None:
                inc_h, inc_by = o.dma[0].h, o.dma[2]
            elif o.sem_val is not None:
                inc_h, inc_by = self.esem[o.eng].h, 1

            def thunk(e, o=o, wl=wl, inc_h=inc_h, inc_by=inc_by):
                for h, v in wl:
                    e.wait_ge(h, v)
                if o.fn is not None:
                    r = o.fn(e)
                    if inc_h is not None:
                        r.then_inc(inc_h, inc_by)
                elif inc_h is not None:
                    e.nop().then_inc(inc_h, inc_by)
            return thunk
        blk = self.ctx.enter_context(self.nc.Block())
        for eng in ENGINES:
            lst = [make_thunk(o) for o in order[eng]]
            if lst:
                getattr(blk, eng)(lambda e, lst=lst: [f(e) for f in lst])


def make_consts(T):
    DH = 64
    c = {}
    c["ident_f"] = np.eye(128, dtype=np.float32)
    c["ident_b"] = np.eye(128, dtype=ml_dtypes.bfloat16)
    pm = np.zeros((128, 128), np.float32)
    for g in (0, 64):
        for a in range(32):
            pm[g + a + 32, g + a] = -1.0
            pm[g + a, g + a + 32] = 1.0
    c["perm_b"] = pm.astype(ml_dtypes.bfloat16)

    half = np.arange(0, DH, 2, dtype=np.float64)
    inv_freq = 1.0 / 10000.0 ** (half / DH)
    t = np.arange(T, dtype=np.float64)
    fr = t[:, None] * inv_freq[None, :]
    fr = np.concatenate([fr, fr], -1)
    base = (half + 0.4 * DH) / (1.4 * DH)
    power = (t - T // 2) / (WIN // 2)
    sc = np.concatenate([base[None, :] ** power[:, None]] * 2, -1)
    cos, sin = np.cos(fr), np.sin(fr)
    tabs = np.stack([cos * sc * QK_SCALE, sin * sc * QK_SCALE, cos / sc, sin / sc])
    tabs = np.transpose(tabs.astype(np.float32), (0, 2, 1))
    c["tabs"] = np.ascontiguousarray(
        tabs.reshape(2, 128, tabs.shape[2])).astype(np.float32)

    p = np.arange(128)[:, None]
    cg = np.arange(384)[None, :]
    c["msk_g"] = np.where((cg >= p) & (cg <= p + 256), 0.0, NEG).astype(ml_dtypes.bfloat16)
    c128 = np.arange(128)[None, :]
    tri = np.where(c128 <= p, 0.0, NEG).astype(np.float32)
    c["msk_w0a"] = tri.astype(ml_dtypes.bfloat16)
    c["msk_w0b"] = np.concatenate([np.zeros((128, 128), np.float32), tri],
                                  axis=1).astype(ml_dtypes.bfloat16)

    eq2 = np.zeros((128, 2), np.float32)
    eq2[0:64, 0] = 1.0
    eq2[64:128, 1] = 1.0
    c["eq2"] = eq2.astype(ml_dtypes.bfloat16)
    e2f = np.zeros((2, 128), np.float32)
    e2f[0, 0:64] = 1.0
    e2f[1, 64:128] = 1.0
    c["e2f"] = e2f

    s1 = np.zeros((128, 33), np.float32)
    s1[:, 0] = 1.0 / 1024.0
    c["st1"] = s1.astype(ml_dtypes.bfloat16)
    s33 = np.zeros((128, 33), np.float32)
    s33[:, 32] = 1.0 / 1024.0
    c["st33"] = s33.astype(ml_dtypes.bfloat16)
    c["ones_row_f"] = np.ones((1, 128), np.float32)
    return c


def build_program(T=2048, NSEQ=2, DEPTH=6, dbg=(), pipe=(1, 1, 1)):
    nc = bass.Bass("TRN2", target_bir_lowering=False)
    NW = T // WIN
    NT128 = T // 128
    NT512 = T // 512

    dt = nc.dram_tensor
    d_x = dt("x", [NSEQ, T, DIM], F32, kind="ExternalInput").ap()
    d_decay = dt("decay_col", [128, 1], F32, kind="ExternalInput").ap()
    d_wT = dt("wT", [DIM, DIM], F32, kind="ExternalInput").ap()
    d_grep = dt("grep", [128, DIM], F32, kind="ExternalInput").ap()
    d_brep = dt("brep", [128, DIM], F32, kind="ExternalInput").ap()
    d_yrep = dt("yrep", [128, DIM], F32, kind="ExternalInput").ap()
    d_wqkv = dt("wqkv", [DEPTH, DIM, 3 * DIM], F32, kind="ExternalInput").ap()
    d_wo = dt("wo", [DEPTH, DIM, DIM], F32, kind="ExternalInput").ap()
    d_w1 = dt("w1", [DEPTH, DIM, 2 * DI], F32, kind="ExternalInput").ap()
    d_w2 = dt("w2", [DEPTH, DI, DIM], F32, kind="ExternalInput").ap()
    d_lp = dt("lp", [DEPTH, 128, 70], F32, kind="ExternalInput").ap()
    d_identf = dt("ident_f", [128, 128], F32, kind="ExternalInput").ap()
    d_identb = dt("ident_b", [128, 128], BF16, kind="ExternalInput").ap()
    d_perm = dt("perm_b", [128, 128], BF16, kind="ExternalInput").ap()
    d_tabs = dt("tabs", [2, 128, T], F32, kind="ExternalInput").ap()
    d_mskg = dt("msk_g", [128, 384], BF16, kind="ExternalInput").ap()
    d_mskw0a = dt("msk_w0a", [128, 128], BF16, kind="ExternalInput").ap()
    d_mskw0b = dt("msk_w0b", [128, 256], BF16, kind="ExternalInput").ap()
    d_eq2 = dt("eq2", [128, 2], BF16, kind="ExternalInput").ap()
    d_e2f = dt("e2f", [2, 128], F32, kind="ExternalInput").ap()
    d_st1 = dt("st1", [128, 33], BF16, kind="ExternalInput").ap()
    d_st33 = dt("st33", [128, 33], BF16, kind="ExternalInput").ap()
    d_onesrow = dt("ones_row_f", [1, 128], F32, kind="ExternalInput").ap()
    d_sscr = dt("sscr", [NSEQ, NT128, 128, DIM], F32,
                kind=("ExternalOutput" if "sscr" in dbg else "Internal")).ap()
    d_out = dt("out", [NSEQ, T, DIM], F32, kind="ExternalOutput").ap()
    d_dbg = {}
    for tag in (t for t in dbg if t != "sscr" and not t.startswith("aw")):
        d_dbg[tag] = dt("dbg_" + tag, [NKT, 128, T], F32, kind="ExternalOutput").ap()
    WDBG = None
    for tag in dbg:
        if tag.startswith("aw"):
            WDBG = int(tag[2:])
            d_dbg["qrot"] = dt("dbg_qrot", [NKT, 128, WIN], F32, kind="ExternalOutput").ap()
            d_dbg["krot"] = dt("dbg_krot", [NKT, 128, 2 * WIN], F32, kind="ExternalOutput").ap()
            d_dbg["vtok"] = dt("dbg_vtok", [4, 128, DIM], F32, kind="ExternalOutput").ap()
            d_dbg["aout"] = dt("dbg_aout", [NKT, 128, WIN], F32, kind="ExternalOutput").ap()

    ctx = ExitStack()
    kb = KB(nc, ctx)
    _uid = [0]

    def _nm(name):
        _uid[0] += 1
        return f"{name}_u{_uid[0]}"
    sbp = lambda name, shape, dtype: ctx.enter_context(nc.sbuf_tensor(name, shape, dtype))

    with ctx:
        xT = [sbp(f"xT{k}", [128, T + 1], F32) for k in range(NKT)]
        tab2 = [sbp(f"tab2_{i}", [128, T], F32) for i in range(2)]
        tabsrc = [(tab2[0], 0), (tab2[0], 64), (tab2[1], 0), (tab2[1], 64)]
        mskg = sbp("mskg", [128, 384], BF16)
        mskw0a = sbp("mskw0a", [128, 128], BF16)
        mskw0b = sbp("mskw0b", [128, 256], BF16)
        identf = sbp("identf", [128, 128], F32)
        identb = sbp("identb", [128, 128], BF16)
        perm = sbp("perm", [128, 128], BF16)
        eq2 = sbp("eq2c", [128, 2], BF16)
        e2f = sbp("e2fc", [2, 128], F32)
        eps24 = sbp("eps24", [128, 1], F32)
        st1 = sbp("st1c", [128, 33], BF16)
        st33 = sbp("st33c", [128, 33], BF16)
        onesrow = sbp("onesrow", [1, 128], F32)
        lp = sbp("lp_sb", [128, 70 * max(DEPTH, 1)], F32)
        decay = sbp("decay", [128, 1], F32)
        epscol = sbp("epscol", [128, 1], F32)
        acol = sbp("acol", [128, 1], F32)
        atile = sbp("atile", [128, 512], F32)
        pb = [ctx.enter_context(nc.psum_tensor(f"pb{i}", [128, 512], F32))
              for i in range(8)]

        kb.init_esems()
        sd_c = kb.sem("sd_c")
        sd_x = kb.sem("sd_x")
        sd_w = kb.sem("sd_w")
        sv = kb.sem("sv")
        sg = kb.sem("sg")

        sa = kb.sem("sa")
        sp = kb.sem("sp")
        st_ = kb.sem("st")
        kb.compute_sems = {sv, sa, sp, sg}
        w1s = [kb.sem("w1s0"), kb.sem("w1s1")]
        sd_par = [kb.sem("sd_par0"), kb.sem("sd_par1")]
        sd_wq = kb.sem("sd_wq")
        sd_wo2 = kb.sem("sd_wo2")
        sd_w2s = kb.sem("sd_w2s")
        sd_wt = kb.sem("sd_wt")
        st_par = [kb.sem("st_par0"), kb.sem("st_par1")]
        wqg = [kb.sem(f"wqg{i}") for i in range(4)]
        w2g = [kb.sem(f"w2g{i}") for i in range(5)]

        for t_sb, t_d in [(identf, d_identf), (identb, d_identb), (perm, d_perm),
                          (mskg, d_mskg), (mskw0a, d_mskw0a), (mskw0b, d_mskw0b),
                          (eq2, d_eq2), (e2f, d_e2f), (st1, d_st1), (st33, d_st33),
                          (onesrow, d_onesrow), (decay, d_decay)]:
            kb.op("sync", lambda e, o=t_sb, i=t_d: e.dma_start(o[:, :], i),
                  inc=sd_c, by=16)
        for i in range(2):
            kb.op("sync", lambda e, o=tab2[i], i_=d_tabs[i]: e.dma_start(o[:, :], i_),
                  inc=sd_c, by=16)
        kb.op("sync", lambda e: e.dma_start(
            lp.ap().rearrange("p (l c) -> p l c", l=DEPTH),
            d_lp.rearrange("l p c -> p l c")), inc=sd_c, by=16)
        CT = (sd_c, sd_c.n)

        def lpc(l, col):
            return lp[:, 70 * l + col: 70 * l + col + 1]

        for k in range(NKT):
            kb.op("vector", lambda e, k=k: e.memset(xT[k][:, 0:1], 0.0), inc=sv)
        t_sig = kb.op("scalar", lambda e: e.activation(
            out=acol[:, 0:1], in_=decay[:, 0:1], func=AF.Sigmoid),
            waits=[CT], inc=sa)
        teps = kb.op("vector", lambda e: e.memset(epscol[:, :], LN_EPS), inc=sv)
        teps24 = kb.op("vector", lambda e: e.memset(eps24[:, :], 1e-24), inc=sv)
        t_at0 = kb.op("vector", lambda e: e.memset(atile[:, :], 1.0), inc=sv)
        t_at = kb.op("vector", lambda e: e.tensor_scalar_mul(
            atile[:, :], atile[:, :], acol[:, 0:1]), waits=[t_sig, t_at0], inc=sv)

        def run_p0(s):
            kb.barrier()
            with ExitStack() as p0:
                sb0 = lambda n_, sh, d_: p0.enter_context(nc.sbuf_tensor(_nm(n_), sh, d_))
                xtok = [sb0(f"p0x{i}", [128, DIM], F32) for i in range(2)]
                xsq = sb0("p0sq", [128, DIM], F32)
                cmu = sb0("p0mu", [128, 1], F32)
                cm2 = sb0("p0m2", [128, 1], F32)
                cvar = sb0("p0var", [128, 1], F32)
                crs = sb0("p0rs", [128, 1], F32)
                xn = sb0("p0xn", [128, DIM], F32)
                prev_done = [None, None]
                prev_txn = None
                prev_tp = None
                prev_p0ev = {}
                for tc in range(NT128):
                    par = tc % 2
                    tl = kb.op("sync", lambda e, tc=tc, par=par: e.dma_start(
                        xtok[par][:, :], d_x[s, tc * 128:(tc + 1) * 128, :]),
                        waits=[prev_done[par]], inc=sd_par[par], by=16)
                    t1 = kb.op("vector", lambda e, par=par: e.tensor_mul(
                        xsq[:, :], xtok[par][:, :], xtok[par][:, :]),
                        waits=[tl, prev_txn], inc=sv)
                    t2 = kb.op("vector", lambda e, par=par: e.reduce_sum(
                        cmu[:, 0:1], xtok[par][:, :], axis=AX.X),
                        waits=[tl, prev_txn], inc=sv)
                    t3 = kb.op("vector", lambda e: e.reduce_sum(
                        cm2[:, 0:1], xsq[:, :], axis=AX.X), waits=[t1], inc=sv)
                    t4 = kb.op("vector", lambda e: e.tensor_scalar_mul(
                        cmu[:, 0:1], cmu[:, 0:1], 1.0 / DIM), waits=[t2], inc=sv)
                    t5 = kb.op("vector", lambda e: e.tensor_scalar(
                        out=cvar[:, 0:1], in0=cmu[:, 0:1], scalar1=cmu[:, 0:1],
                        scalar2=-1.0, op0=ALU.mult, op1=ALU.mult), waits=[t4], inc=sv)
                    t6 = kb.op("vector", lambda e: e.tensor_scalar(
                        out=cvar[:, 0:1], in0=cm2[:, 0:1], scalar1=1.0 / DIM,
                        scalar2=cvar[:, 0:1], op0=ALU.mult, op1=ALU.add),
                        waits=[t3, t5], inc=sv)
                    t7 = kb.op("scalar", lambda e: e.activation(
                        out=crs[:, 0:1], in_=cvar[:, 0:1], func=AF.Ln,
                        bias=epscol[:, 0:1]), waits=[t6, teps], inc=sa)
                    t8 = kb.op("scalar", lambda e: e.activation(
                        out=crs[:, 0:1], in_=crs[:, 0:1], func=AF.Exp, scale=-0.5),
                        waits=[t7], inc=sa)
                    tneg = kb.op("vector", lambda e: e.tensor_scalar_mul(
                        cmu[:, 0:1], cmu[:, 0:1], -1.0), waits=[t5], inc=sv)
                    txn = kb.op("vector", lambda e, par=par: e.tensor_scalar(
                        out=xn[:, :], in0=xtok[par][:, :], scalar1=cmu[:, 0:1],
                        scalar2=crs[:, 0:1], op0=ALU.add, op1=ALU.mult),
                        waits=[t8, tneg, prev_tp], inc=sv)
                    evs = {}
                    for k in range(NKT):
                        bank = pb[k % 2]
                        tp = kb.op("tensor", lambda e, k=k, bank=bank: e.transpose(
                            bank[:, 0:128], xn[:, k * 128:(k + 1) * 128], identf[:, :]),
                            waits=[txn, evs.get(k - 2), prev_p0ev.get(par)], inc=sp)
                        evs[k] = kb.op("scalar", lambda e, k=k, tc=tc, bank=bank: e.copy(
                            xT[k][:, 1 + tc * 128: 1 + (tc + 1) * 128], bank[:, 0:128]),
                            waits=[tp], inc=sa)
                        prev_tp = tp
                    prev_p0ev[par] = evs[NKT - 1]
                    prev_done[par] = evs[NKT - 1]
                    prev_txn = txn

        def run_sbranch(s):
            kb.barrier()
            with ExitStack() as p1:
                sb1 = lambda n_, sh, d_: p1.enter_context(nc.sbuf_tensor(_nm(n_), sh, d_))
                S_bf = [sb1(f"Sbf{k}", [128, T], BF16) for k in range(NKT)]
                scanf = [sb1(f"scanf{p}", [128, T], F32) for p in range(2)]
                wT_bf = sb1("wTbf", [128, NKT * DIM], BF16)
                grep = sb1("grep_s", [128, DIM], F32)
                brep = sb1("brep_s", [128, DIM], F32)
                yrep = sb1("yrep_s", [128, DIM], F32)
                yf = sb1("yf", [128, DIM], F32)
                ysq = sb1("ysq", [128, DIM], F32)
                sout = sb1("sout", [128, DIM], F32)
                cmu = sb1("s_mu", [128, 1], F32)
                cm2 = sb1("s_m2", [128, 1], F32)
                cvar = sb1("s_var", [128, 1], F32)
                crs = sb1("s_rs", [128, 1], F32)
                if s == 0:
                    print("SBUF watermark (sbranch):", nc.sbuf_base)

                tw = kb.op("gpsimd", lambda e: e.dma_start(
                    wT_bf.ap().rearrange("p (kc e) -> p kc e", kc=NKT),
                    d_wT.rearrange("(kc p) e -> p kc e", p=128)),
                    inc=sd_wt, by=16)
                kb.op("sync", lambda e: e.dma_start(grep[:, :], d_grep),
                      inc=sd_x, by=16)
                kb.op("sync", lambda e: e.dma_start(brep[:, :], d_brep),
                      inc=sd_x, by=16)
                trep = kb.op("sync", lambda e: e.dma_start(yrep[:, :], d_yrep),
                             inc=sd_x, by=16)
                tg = tb = ty = trep

                sbf_toks = [None] * NKT
                last_k = {}
                for k in range(NKT):
                    p = k % 2
                    tprev = last_k.get(k - 2, t_at)
                    for ch in range(T // 512):
                        init = 0.0 if ch == 0 else scanf[p][:, ch * 512 - 1: ch * 512]
                        tprev = kb.op("vector", lambda e, k=k, ch=ch, init=init, p=p: e.tensor_tensor_scan(
                            out=scanf[p][:, ch * 512:(ch + 1) * 512],
                            data0=atile[:, :],
                            data1=xT[k][:, 1 + ch * 512: 1 + (ch + 1) * 512],
                            initial=init, op0=ALU.mult, op1=ALU.add),
                            waits=[tprev, t_at], inc=sv)
                    tprev = kb.op("vector", lambda e, k=k, p=p: e.tensor_copy(
                        S_bf[k][:, :], scanf[p][:, :]), waits=[tprev], inc=sv)
                    last_k[k] = tprev
                    sbf_toks[k] = tprev

                bank_free = [None, None]
                yf_free = None
                prev_tsp = None
                for tc in range(NT128):
                    ytoks = []
                    for eh in range(2):
                        bank = pb[2 + eh]
                        lastmm = None
                        for k in range(NKT):
                            lastmm = kb.op("tensor", lambda e, k=k, eh=eh, tc=tc, bank=bank: e.matmul(
                                bank[:, :],
                                S_bf[k][:, tc * 128:(tc + 1) * 128],
                                wT_bf[:, k * DIM + eh * 512: k * DIM + (eh + 1) * 512],
                                start=(k == 0), stop=(k == NKT - 1)),
                                waits=[sbf_toks[k], tw, bank_free[eh]] if k == 0
                                else [sbf_toks[k], ("soft", lastmm)],
                                inc=(sp if k == NKT - 1 else None))
                        tev = kb.op("vector", lambda e, eh=eh, bank=bank: e.scalar_tensor_tensor(
                            out=yf[:, eh * 512:(eh + 1) * 512], in0=bank[:, :],
                            scalar=1.0, in1=yrep[:, eh * 512:(eh + 1) * 512],
                            op0=ALU.mult, op1=ALU.add),
                            waits=[lastmm, ty, yf_free], inc=sv)
                        bank_free[eh] = tev
                        ytoks.append(tev)
                    t1 = kb.op("vector", lambda e: e.tensor_mul(
                        ysq[:, :], yf[:, :], yf[:, :]), waits=ytoks, inc=sv)
                    t2 = kb.op("vector", lambda e: e.reduce_sum(
                        cmu[:, 0:1], yf[:, :], axis=AX.X), waits=ytoks, inc=sv)
                    t3 = kb.op("vector", lambda e: e.reduce_sum(
                        cm2[:, 0:1], ysq[:, :], axis=AX.X), waits=[t1], inc=sv)
                    t4 = kb.op("vector", lambda e: e.tensor_scalar_mul(
                        cmu[:, 0:1], cmu[:, 0:1], 1.0 / DIM), waits=[t2], inc=sv)
                    t5 = kb.op("vector", lambda e: e.tensor_scalar(
                        out=cvar[:, 0:1], in0=cmu[:, 0:1], scalar1=cmu[:, 0:1],
                        scalar2=-1.0, op0=ALU.mult, op1=ALU.mult), waits=[t4], inc=sv)
                    t6 = kb.op("vector", lambda e: e.tensor_scalar(
                        out=cvar[:, 0:1], in0=cm2[:, 0:1], scalar1=1.0 / DIM,
                        scalar2=cvar[:, 0:1], op0=ALU.mult, op1=ALU.add),
                        waits=[t3, t5], inc=sv)
                    t7 = kb.op("scalar", lambda e: e.activation(
                        out=crs[:, 0:1], in_=cvar[:, 0:1], func=AF.Ln,
                        bias=epscol[:, 0:1]), waits=[t6, teps], inc=sa)
                    t8 = kb.op("scalar", lambda e: e.activation(
                        out=crs[:, 0:1], in_=crs[:, 0:1], func=AF.Exp, scale=-0.5),
                        waits=[t7], inc=sa)
                    tneg = kb.op("vector", lambda e: e.tensor_scalar_mul(
                        cmu[:, 0:1], cmu[:, 0:1], -1.0), waits=[t5], inc=sv)
                    ta = kb.op("vector", lambda e: e.tensor_scalar(
                        out=sout[:, :], in0=yf[:, :], scalar1=cmu[:, 0:1],
                        scalar2=crs[:, 0:1], op0=ALU.add, op1=ALU.mult),
                        waits=[t8, tneg, prev_tsp], inc=sv)
                    tb2 = kb.op("vector", lambda e: e.tensor_mul(
                        sout[:, :], sout[:, :], grep[:, :]), waits=[ta, tg], inc=sv)
                    tc2 = kb.op("vector", lambda e: e.tensor_add(
                        sout[:, :], sout[:, :], brep[:, :]), waits=[tb2, tb], inc=sv)
                    prev_tsp = kb.op("sync", lambda e, tc=tc: e.dma_start(
                        d_sscr[s, tc], sout[:, :]), waits=[tc2], inc=sd_x, by=16)
                    yf_free = ta

        def run_attn(s, l):
            kb.barrier()
            with ExitStack() as pa:
                sba = lambda n_, sh, d_: pa.enter_context(nc.sbuf_tensor(_nm(n_), sh, d_))
                wqkv = sba("wqkv", [128, NKT * 3 * DIM], BF16)
                wo = sba("wo_l", [128, NKT * DIM], BF16)
                xbfw = [sba(f"xbfw{k}", [128, WIN + 1], BF16) for k in range(NKT)]
                q_rot = [sba(f"qrot{k}", [128, WIN], BF16) for k in range(NKT)]
                k_buf = [sba(f"kbuf{k}", [128, 2 * WIN], BF16) for k in range(NKT)]
                v_buf = [sba(f"vbuf{j}", [128, DIM], BF16) for j in range(4)]
                qbfe = [sba(f"qbfe{p}", [128, WIN], BF16) for p in range(2)]
                q2b = [sba(f"q2b{p}", [128, WIN], BF16) for p in range(2)]
                fscr = [sba(f"fscr{p}", [128, 1152], F32) for p in range(2)]
                qf = [fscr[p][:, 0:WIN] for p in range(2)]
                rotA = [fscr[p][:, WIN:2 * WIN] for p in range(2)]
                rotB = [fscr[p][:, 2 * WIN:3 * WIN] for p in range(2)]
                msk_sc = [[fscr[0][:, 0:384], fscr[0][:, 384:768]],
                          [fscr[1][:, 0:384], fscr[1][:, 384:768]],
                          [fscr[0][:, 768:1152], fscr[1][:, 768:1152]]]
                ssr = [sba(f"ssr{p}", [2, WIN], F32) for p in range(2)]
                twt = [sba(f"twt{i}", [128, WIN], F32) for i in range(4)]
                Pbf = [[sba(f"Pbf{p}{ic}", [128, 384], BF16) for ic in range(2)]
                       for p in range(3)]
                PTbf = [sba(f"PTbf{j}", [128, WIN], BF16) for j in range(4)]
                negmax = [sba(f"negmax{p}", [128, 2], F32) for p in range(3)]
                rsum = [sba(f"rsum{p}", [128, 2], F32) for p in range(3)]
                rrec = [sba(f"rrec{p}", [128, 2], F32) for p in range(3)]
                Dbf = [sba(f"Dbf{p}", [128, 512], BF16) for p in range(3)]
                aout = [sba(f"aout{k}", [128, WIN], BF16) for k in range(NKT)]
                xbnd = sba("xbnd", [128, 8], F32)
                if s == 0 and l == 0:
                    print("SBUF watermark (attn):", nc.sbuf_base)

                t_dz = [kb.op("vector", lambda e, p=p: e.memset(Dbf[p][:, :], 0.0),
                              inc=sv) for p in range(3)]
                twq_g = []
                for g in range(4):
                    twq_g.append(kb.op("gpsimd", lambda e, g=g: e.dma_start(
                        wqkv.ap().rearrange("p (kc o) -> p kc o", kc=NKT)[:, 2 * g:2 * g + 2, :],
                        d_wqkv[l].rearrange("(kc p) o -> p kc o", p=128)[:, 2 * g:2 * g + 2, :]),
                        inc=wqg[g], by=16))
                twq_c = [twq_g[ci // 2] for ci in range(NKT)]
                twq = twq_g[-1]
                two = kb.op("gpsimd", lambda e: e.dma_start(
                    wo.ap().rearrange("p (kc o) -> p kc o", kc=NKT),
                    d_wo[l].rearrange("(kc p) o -> p kc o", p=128)),
                    inc=sd_wo2, by=16)

                def qk_stage(idx, side, kt, wb, lastmm, free_tok, sh_toks, twt_tok):
                    p = idx % 2
                    ti = (0, 1) if side == "q" else (2, 3)
                    qscol = lpc(l, 68) if side == "q" else lpc(l, 69)
                    bank = pb[p]
                    psA = pb[4 + 2 * p]
                    psB = pb[5 + 2 * p]
                    t_e = kb.op("scalar", lambda e, bank=bank, qscol=qscol, p=p: e.activation(
                        out=qbfe[p][:, :], in_=bank[:, 0:WIN], func=AF.Copy,
                        scale=qscol), waits=[lastmm, free_tok], inc=sa)
                    t_ef = kb.op("scalar", lambda e, bank=bank, qscol=qscol, p=p: e.activation(
                        out=qf[p], in_=bank[:, 0:WIN], func=AF.Copy,
                        scale=qscol), waits=[lastmm, free_tok], inc=sa)
                    t_s = kb.op("scalar", lambda e, bank=bank, p=p: e.activation(
                        out=q2b[p][:, :], in_=bank[:, 0:WIN], func=AF.Square),
                        waits=[lastmm, free_tok], inc=sa)
                    t_ss = kb.op("tensor", lambda e, p=p, psA=psA: e.matmul(
                        psA[0:2, 0:WIN], eq2[:, :], q2b[p][:, :],
                        start=True, stop=True), waits=[t_s, free_tok, CT], inc=sp)
                    t_l = kb.op("scalar", lambda e, p=p, psA=psA: e.activation(
                        out=ssr[p][:, :], in_=psA[0:2, 0:WIN], func=AF.Ln,
                        bias=eps24[0:2, 0:1]),
                        waits=[t_ss, free_tok, teps24], inc=sa)
                    t_x = kb.op("scalar", lambda e, p=p: e.activation(
                        out=ssr[p][:, :], in_=ssr[p][:, :], func=AF.Exp, scale=-0.5),
                        waits=[t_l], inc=sa)
                    t_bb = kb.op("tensor", lambda e, p=p, psB=psB: e.matmul(
                        psB[:, 0:WIN], e2f[:, :], ssr[p][:, :],
                        start=True, stop=True), waits=[t_x, free_tok, CT], inc=sp)
                    t_R = kb.op("tensor", lambda e, p=p, psA=psA: e.matmul(
                        psA[:, 0:WIN], perm[:, :], qbfe[p][:, :],
                        start=True, stop=True), waits=[t_e, t_l], inc=sp)
                    t_A = kb.op("vector", lambda e, p=p, i0=ti[0]: e.tensor_mul(
                        rotA[p], qf[p], twt[i0][:, :]),
                        waits=[t_ef, free_tok, twt_tok], inc=sv)
                    t_B = kb.op("vector", lambda e, p=p, i1=ti[1], psA=psA: e.tensor_mul(
                        rotB[p], psA[:, 0:WIN], twt[i1][:, :]),
                        waits=[t_R, twt_tok], inc=sv)
                    t_C = kb.op("vector", lambda e, p=p: e.tensor_add(
                        rotA[p], rotA[p], rotB[p]),
                        waits=[t_A, t_B], inc=sv)
                    if side == "q":
                        dst = q_rot[kt][:, :]
                    else:
                        dst = k_buf[kt][:, WIN:2 * WIN]
                    return kb.op("vector", lambda e, dst=dst, p=p, psB=psB: e.tensor_mul(
                        dst, rotA[p], psB[:, 0:WIN]),
                        waits=[t_C, t_bb] + sh_toks, inc=sv)

                def head_A(w, h, qk_all, afree):
                    hp, par = h // 2, h % 2
                    p = h % 3
                    h0 = par * 64
                    sim_bank = [pb[0], pb[1], pb[4]][p]
                    if w == 0:
                        live = [(0, 256, 128), (1, 256, 256)]
                        masks = [mskw0a, mskw0b]
                    else:
                        live = [(0, 0, 384), (1, 128, 384)]
                        masks = [mskg, mskg]
                    exp_toks = []
                    for (ic, jlo, width), msk in zip(live, masks):
                        t_sim = kb.op("tensor", lambda e, ic=ic, jlo=jlo, width=width, h0=h0, hp=hp, sim_bank=sim_bank: e.matmul(
                            sim_bank[:, 0:width],
                            q_rot[hp][h0:h0 + 64, ic * 128:(ic + 1) * 128],
                            k_buf[hp][h0:h0 + 64, jlo:jlo + width],
                            start=True, stop=True),
                            waits=qk_all + exp_toks + [afree], inc=sp)
                        t_msk = kb.op("vector", lambda e, ic=ic, width=width, msk=msk, sim_bank=sim_bank, p=p: e.tensor_tensor(
                            out=msk_sc[p][ic][:, 0:width], in0=sim_bank[:, 0:width],
                            in1=msk[:, 0:width], op=ALU.add),
                            waits=[t_sim, afree], inc=sv)
                        t_mx = kb.op("vector", lambda e, ic=ic, width=width, p=p: e.reduce_max(
                            out=negmax[p][:, ic:ic + 1], in_=msk_sc[p][ic][:, 0:width],
                            axis=AX.X, negate=True), waits=[t_msk], inc=sv)
                        t_ex = kb.op("scalar", lambda e, ic=ic, width=width, p=p: e.activation(
                            out=Pbf[p][ic][:, 0:width], in_=msk_sc[p][ic][:, 0:width],
                            func=AF.Exp, bias=negmax[p][:, ic:ic + 1],
                            accum_out=rsum[p][:, ic:ic + 1]),
                            waits=[t_mx, afree], inc=sa)
                        exp_toks.append(t_ex)
                    t_rc = kb.op("vector", lambda e, p=p: e.reciprocal(
                        rrec[p][:, 0:2], rsum[p][:, 0:2]), waits=exp_toks, inc=sv)
                    d_toks = []
                    for ic, dlo in ((0, 0), (1, 384)):
                        d_toks.append(kb.op("vector", lambda e, ic=ic, dlo=dlo, p=p: e.tensor_scalar_mul(
                            Dbf[p][:, dlo:dlo + 128], identb[:, :],
                            rrec[p][:, ic:ic + 1]),
                            waits=[t_rc, afree] + t_dz, inc=sv))
                    return exp_toks, d_toks

                def head_B(w, h, stA, prev_ptev, outfree, pvfree):
                    hp, par = h // 2, h % 2
                    p = h % 3
                    h0 = par * 64
                    ovb = pb[5] if par == 0 else pb[7]
                    exp_toks, d_toks = stA
                    if w == 0:
                        pt_plan = {2: [(0, 0)], 3: [(1, 0)]}
                        pt_plan = {2: [(0, 0), (1, 0)], 3: [(1, 128)]}
                        pv_plan = [2, 3]
                    else:
                        pt_plan = {0: [(0, 0)], 1: [(0, 128), (1, 0)],
                                   2: [(0, 256), (1, 128)], 3: [(1, 256)]}
                        pv_plan = [0, 1, 2, 3]
                    pt_evac = {}
                    lastpt = None
                    for jc, contribs in pt_plan.items():
                        bank = pb[2 + (jc % 2)]
                        off = (jc // 2) * 256
                        if jc - 2 in pt_evac:
                            bfree = pt_evac[jc - 2]
                        elif prev_ptev:
                            bfree = prev_ptev.get(jc + 2) or prev_ptev.get(jc)
                        else:
                            bfree = None
                        for n_, (ic, ploc) in enumerate(contribs):
                            lastpt = kb.op("tensor", lambda e, ic=ic, ploc=ploc, off=off, n_=n_, bank=bank, p=p, contribs=contribs: e.matmul(
                                bank[:, off: off + 256],
                                Pbf[p][ic][:, ploc:ploc + 128],
                                Dbf[p][:, ic * 256:(ic + 1) * 256],
                                start=(n_ == 0), stop=(n_ == len(contribs) - 1)),
                                waits=exp_toks + d_toks + [lastpt, bfree], inc=sp)
                        if jc % 2 == 0:
                            pt_evac[jc] = kb.op("scalar", lambda e, jc=jc, off=off, bank=bank: e.copy(
                                PTbf[jc][:, :], bank[:, off: off + 256]),
                                waits=[lastpt, pvfree], inc=sa)
                        else:
                            pt_evac[jc] = kb.op("vector", lambda e, jc=jc, off=off, bank=bank: e.tensor_copy(
                                PTbf[jc][:, :], bank[:, off: off + 256]),
                                waits=[lastpt, pvfree], inc=sv)
                    lastpv = None
                    for n_, jc in enumerate(pv_plan):
                        lastpv = kb.op("tensor", lambda e, jc=jc, h0=h0, hp=hp, par=par, n_=n_, ovb=ovb: e.matmul(
                            ovb[h0:h0 + 64, 0:WIN],
                            v_buf[jc][:, (2 * hp + par) * 64: (2 * hp + par) * 64 + 64],
                            PTbf[jc][:, :],
                            start=(n_ == 0), stop=(n_ == len(pv_plan) - 1),
                            tile_position=(0, h0)),
                            waits=(list(pt_evac.values()) + [outfree, lastpv]) if n_ == 0 else [("soft", lastpv)],
                            inc=(sp if n_ == len(pv_plan) - 1 else None))
                    t_ev = kb.op("scalar", lambda e, hp=hp, h0=h0, ovb=ovb: e.copy(
                        aout[hp][h0:h0 + 64, :], ovb[h0:h0 + 64, :WIN]),
                        waits=[lastpv], inc=sa)
                    return max(pt_evac), lastpt, pt_evac, t_ev, lastpv

                shift_src = None
                wo_done = None
                for w in range(NW):
                    wb = w * WIN
                    sh_toks = []
                    if w > 0:
                        for k in range(NKT):
                            sh_toks.append(kb.op("vector", lambda e, k=k: e.tensor_copy(
                                k_buf[k][:, 0:WIN], k_buf[k][:, WIN:2 * WIN]),
                                waits=[shift_src], inc=sv))
                        for j in range(2):
                            sh_toks.append(kb.op("vector", lambda e, j=j: e.tensor_copy(
                                v_buf[j][:, :], v_buf[2 + j][:, :]),
                                waits=[shift_src], inc=sv))
                    xb_toks = []
                    for k in range(NKT):
                        xb_toks.append(kb.op("scalar", lambda e, k=k, wb=wb: e.copy(
                            xbfw[k][:, :], xT[k][:, wb:wb + WIN + 1]),
                            waits=[wo_done], inc=sa))
                    if w > 0:
                        for k in range(4, NKT):
                            xb_toks.append(kb.op("scalar", lambda e, k=k: e.copy(
                                xbfw[k][:, 0:1], xbnd[:, k:k + 1]),
                                waits=[xb_toks[k], snap_toks[k - 4]], inc=sa))
                    twt_tok = None
                    for i in range(4):
                        tsrc, tbase = tabsrc[i]
                        for hh in range(2):
                            twt_tok = kb.op("vector", lambda e, i=i, hh=hh, wb=wb, tsrc=tsrc, tbase=tbase: e.tensor_copy(
                                twt[i][hh * 64:(hh + 1) * 64, :],
                                tsrc[tbase:tbase + 64, wb:wb + WIN]),
                                waits=[CT, wo_done], inc=sv)
                    insts = [(side, kt, obase) for side, obase in
                             (("q", 0), ("k", DIM)) for kt in range(NKT)]
                    d_tok = {}
                    qk_toks = []
                    for idx, (side, kt, obase) in enumerate(insts):
                        p = idx % 2
                        bank = pb[p]
                        free_tok = d_tok.get(idx - 2 if pipe[1] else idx - 1)
                        lastmm = None
                        for ci in range(NKT):
                            lastmm = kb.op("tensor", lambda e, kt=kt, ci=ci, obase=obase, bank=bank: e.matmul(
                                bank[:, 0:WIN],
                                wqkv[:, ci * 3 * DIM + obase + kt * 128: ci * 3 * DIM + obase + (kt + 1) * 128],
                                xbfw[ci][:, (1 if ci < 4 else 0): (1 if ci < 4 else 0) + WIN],
                                start=(ci == 0), stop=(ci == NKT - 1)),
                                waits=(xb_toks + [twq_c[ci], free_tok]) if ci == 0 else [twq_c[ci], ("soft", lastmm)],
                                inc=(sp if ci == NKT - 1 else None))
                        d_tok[idx] = qk_stage(idx, side, kt, wb, lastmm, free_tok,
                                              sh_toks, twt_tok)
                        qk_toks.append(d_tok[idx])
                    v_toks = []
                    for th in range(2):
                        lastmm = [None, None]
                        for ci in range(NKT):
                            for eh in range(2):
                                lastmm[eh] = kb.op("tensor", lambda e, th=th, ci=ci, eh=eh: e.matmul(
                                    pb[2 + eh][:, :],
                                    xbfw[ci][:, (1 if ci < 4 else 0) + th * 128: (1 if ci < 4 else 0) + (th + 1) * 128],
                                    wqkv[:, ci * 3 * DIM + 2 * DIM + eh * 512: ci * 3 * DIM + 2 * DIM + (eh + 1) * 512],
                                    start=(ci == 0), stop=(ci == NKT - 1)),
                                    waits=(xb_toks + [twq_c[ci]] + sh_toks + v_toks) if ci == 0 else [twq_c[ci], ("soft", lastmm[eh])],
                                    inc=(sp if ci == NKT - 1 else None))
                        for eh in range(2):
                            v_toks.append(kb.op("scalar", lambda e, th=th, eh=eh: e.copy(
                                v_buf[2 + th][:, eh * 512:(eh + 1) * 512],
                                pb[2 + eh][:, :]), waits=[lastmm[eh]], inc=sa))
                    qk_all = qk_toks + v_toks + sh_toks
                    if WDBG is not None and s == 0 and l == 0 and w == WDBG:
                        for k in range(NKT):
                            kb.op("gpsimd", lambda e, k=k: e.dma_start(
                                d_dbg["qrot"][k], q_rot[k][:, :]), waits=qk_all,
                                inc=st_, by=16)
                            kb.op("gpsimd", lambda e, k=k: e.dma_start(
                                d_dbg["krot"][k], k_buf[k][:, :]), waits=qk_all,
                                inc=st_, by=16)
                        for j in range(4):
                            kb.op("gpsimd", lambda e, j=j: e.dma_start(
                                d_dbg["vtok"][j], v_buf[j][:, :]), waits=qk_all,
                                inc=st_, by=16)
                    stA = {}
                    Bres = {}
                    ao_evacs = []
                    if pipe[0]:
                        stA[0] = head_A(w, 0, qk_all, None)
                        stA[1] = head_A(w, 1, qk_all, None)
                        stA[2] = head_A(w, 2, qk_all, None)
                        for h in range(16):
                            pt_prev = Bres.get(h - 1)
                            prev_ptev = pt_prev[2] if pt_prev else None
                            out_prev = Bres.get(h - 2)
                            outfree = out_prev[3] if out_prev else None
                            pv_prev = Bres.get(h - 1)
                            pvfree = pv_prev[4] if pv_prev else None
                            Bres[h] = head_B(w, h, stA[h], prev_ptev, outfree,
                                             pvfree)
                            ao_evacs.append(Bres[h][3])
                            if h + 3 < 16:
                                stA[h + 3] = head_A(w, h + 3, qk_all, Bres[h][1])
                    else:
                        prev_ev = None
                        for h in range(16):
                            stA[h] = head_A(w, h, qk_all, prev_ev)
                            pt_prev = Bres.get(h - 1)
                            prev_ptev = pt_prev[2] if pt_prev else None
                            pvfree = pt_prev[4] if pt_prev else None
                            Bres[h] = head_B(w, h, stA[h], prev_ptev, prev_ev,
                                             pvfree)
                            prev_ev = Bres[h][3]
                            ao_evacs.append(prev_ev)
                    if WDBG is not None and s == 0 and l == 0 and w == WDBG:
                        for k in range(NKT):
                            kb.op("gpsimd", lambda e, k=k: e.dma_start(
                                d_dbg["aout"][k], aout[k][:, :]), waits=ao_evacs,
                                inc=st_, by=16)
                    snap_toks = []
                    for k in range(4, NKT):
                        snap_toks.append(kb.op("vector", lambda e, k=k, wb=wb: e.tensor_copy(
                            xbnd[:, k:k + 1], xT[k][:, wb + WIN:wb + WIN + 1]),
                            waits=[ao_evacs[-1]], inc=sv))
                    wo_toks = []
                    for kt in range(NKT):
                        bank = pb[6] if kt % 2 == 0 else pb[5]
                        lastmm = None
                        for ci in range(NKT):
                            lastmm = kb.op("tensor", lambda e, kt=kt, ci=ci, bank=bank: e.matmul(
                                bank[:, 0:WIN],
                                wo[:, ci * DIM + kt * 128: ci * DIM + (kt + 1) * 128],
                                aout[ci][:, :],
                                start=(ci == 0), stop=(ci == NKT - 1)),
                                waits=(ao_evacs + [two] + wo_toks[-2:]) if ci == 0 else [("soft", lastmm)],
                                inc=(sp if ci == NKT - 1 else None))
                        wo_toks.append(kb.op("vector", lambda e, kt=kt, wb=wb, bank=bank: e.tensor_add(
                            xT[kt][:, 1 + wb: 1 + wb + WIN],
                            bank[:, 0:WIN], xT[kt][:, 1 + wb: 1 + wb + WIN]),
                            waits=[lastmm] + snap_toks, inc=sv))
                    wo_done = wo_toks[-1]
                    shift_src = wo_done
                if f"L{l}a" in dbg:
                    dump_x(s, f"L{l}a", wo_done)

        def run_ffn(s, l):
            kb.barrier()
            with ExitStack() as pf:
                sbf = lambda n_, sh, d_: pf.enter_context(nc.sbuf_tensor(_nm(n_), sh, d_))
                w2 = sbf("w2", [128, NPAIR * DIM], BF16)
                w1p = [sbf(f"w1p{i}", [128, NKT * 256], BF16) for i in range(3)]
                xnbf = [sbf(f"xnbf{k}", [128, 512], BF16) for k in range(NKT)]
                h2 = [sbf(f"h2{i}", [128, NPAIR * 512], BF16) for i in range(2)]
                gel = [sbf(f"gel{p}", [128, 512], BF16) for p in range(2)]
                rtmp = sbf("rtmp", [1, 512], F32)
                rnegmu = sbf("rnegmu", [1, 512], F32)
                rrs = rnegmu
                xbf_s = sbf("xbf_s", [128, 512], BF16)
                xsq_s = sbf("xsq_s", [128, 512], BF16)
                t0f = [sbf(f"t0f{p}", [128, 512], F32) for p in range(2)]
                if s == 0 and l == 0:
                    print("SBUF watermark (ffn):", nc.sbuf_base)

                t_z = kb.op("vector", lambda e: e.memset(
                    w2[:, 21 * DIM:22 * DIM], 0.0), inc=sv)
                groups = [(0, 6), (6, 12), (12, 18), (18, 21)]
                tw2_c = [None] * NPAIR
                for gi, (a_, b_) in enumerate(groups):
                    tok = kb.op("gpsimd", lambda e, a_=a_, b_=b_: e.dma_start(
                        w2.ap().rearrange("p (kc o) -> p kc o", kc=NPAIR)[:, a_:b_, :],
                        d_w2[l, 0:21 * 128, :].rearrange("(kc p) o -> p kc o", p=128)[:, a_:b_, :]),
                        inc=w2g[gi], by=16)
                    for kchunk in range(a_, b_):
                        tw2_c[kchunk] = tok
                rows = DI - 21 * 128
                tw2_c[21] = kb.op("gpsimd", lambda e, rows=rows: e.dma_start(
                    w2[0:rows, 21 * DIM:22 * DIM],
                    d_w2[l, 21 * 128: 21 * 128 + rows, :]),
                    waits=[t_z], inc=w2g[4], by=16)
                tw2 = tw2_c[21]
                w1_free = [None, None, None]

                def load_pair(j):
                    par = j % 3
                    cols = min(128, DI - j * 128)
                    tok = None
                    for half, cbase in ((0, j * 128), (1, DI + j * 128)):
                        tok = kb.op("gpsimd", lambda e, par=par, cbase=cbase, half=half, cols=cols: e.dma_start(
                            w1p[par].ap().rearrange("p (kc c) -> p kc c", kc=NKT)[:, :, half * 128: half * 128 + cols],
                            d_w1[l].rearrange("(kc p) o -> p kc o", p=128)[:, :, cbase:cbase + cols]),
                            waits=[w1_free[par]], inc=w1s[par % 2], by=16)
                    return tok

                rows_free = None
                bcast_free = [None, None]
                stats_free = None
                conv_free = None
                xn_frees = [None]

                def stats_apply(tc):
                    nonlocal rows_free, bcast_free, stats_free, conv_free
                    stats_last = None
                    sq_last = conv_free
                    for k in range(NKT):
                        off = (1 if k < 4 else 0) + tc * 512
                        t_c1 = kb.op("scalar", lambda e, k=k, off=off: e.copy(
                            xbf_s[:, :], xT[k][:, off:off + 512]),
                            waits=[sq_last], inc=sa)
                        t_c2 = kb.op("scalar", lambda e, k=k, off=off: e.activation(
                            out=xsq_s[:, :], in_=xT[k][:, off:off + 512],
                            func=AF.Square), waits=[t_c1], inc=sa)
                        t_m1 = kb.op("tensor", lambda e, k=k: e.matmul(
                            pb[2][0:33, :], st1[:, :], xbf_s[:, :],
                            start=(k == 0), stop=False, skip_group_check=True),
                            waits=[t_c1, ("soft", stats_last)] +
                                  ([stats_free] if k == 0 else []), inc=sp)
                        t_m2 = kb.op("tensor", lambda e, k=k: e.matmul(
                            pb[2][0:33, :], st33[:, :], xsq_s[:, :],
                            start=False, stop=(k == NKT - 1), skip_group_check=True),
                            waits=[t_c2, ("soft", t_m1)], inc=sp)
                        stats_last = t_m2
                        sq_last = t_m2
                    conv_free = stats_last
                    t1 = kb.op("vector", lambda e: e.tensor_scalar_mul(
                        rnegmu[0:1, :], pb[2][0:1, :], -1.0),
                        waits=[stats_last, rows_free], inc=sv)
                    t6 = kb.op("tensor", lambda e: e.matmul(
                        pb[0][:, :], onesrow[0:1, :], rnegmu[0:1, :],
                        start=True, stop=True),
                        waits=[t1, bcast_free[0], CT], inc=sp)
                    t2 = kb.op("vector", lambda e: e.tensor_mul(
                        rtmp[0:1, :], rnegmu[0:1, :], rnegmu[0:1, :]),
                        waits=[t1], inc=sv)
                    t3 = kb.op("vector", lambda e: e.tensor_sub(
                        rtmp[0:1, :], pb[2][32:33, :], rtmp[0:1, :]),
                        waits=[t2], inc=sv)
                    t4 = kb.op("scalar", lambda e: e.activation(
                        out=rrs[0:1, :], in_=rtmp[0:1, :], func=AF.Ln,
                        bias=epscol[0:1, 0:1]), waits=[t3, teps, t6], inc=sa)
                    t5 = kb.op("scalar", lambda e: e.activation(
                        out=rrs[0:1, :], in_=rrs[0:1, :], func=AF.Exp, scale=-0.5),
                        waits=[t4], inc=sa)
                    t7 = kb.op("tensor", lambda e: e.matmul(
                        pb[1][:, :], onesrow[0:1, :], rrs[0:1, :],
                        start=True, stop=True),
                        waits=[t5, bcast_free[1]], inc=sp)
                    rows_free = t7
                    xn_toks = []
                    ap_last = {}
                    for k in range(NKT):
                        off = (1 if k < 4 else 0) + tc * 512
                        p = k % 2
                        ta = kb.op("vector", lambda e, k=k, off=off, p=p: e.tensor_add(
                            t0f[p][:, :], xT[k][:, off:off + 512], pb[0][:, :]),
                            waits=[t6, ap_last.get(k - 2)] + xn_frees, inc=sv)
                        tb_ = kb.op("vector", lambda e, p=p: e.tensor_mul(
                            t0f[p][:, :], t0f[p][:, :], pb[1][:, :]),
                            waits=[ta, t7], inc=sv)
                        tcx = kb.op("vector", lambda e, k=k, l=l, p=p: e.tensor_scalar(
                            out=xnbf[k][:, :], in0=t0f[p][:, :], scalar1=lpc(l, k),
                            scalar2=lpc(l, 8 + k), op0=ALU.mult, op1=ALU.add),
                            waits=[tb_], inc=sv)
                        xn_toks.append(tcx)
                        ap_last[k] = tcx
                    return xn_toks

                tcs = list(reversed(range(NT512)))
                xn_toks = stats_apply(tcs[0])
                evac_hist = []
                h2_guard = [None, None]
                for i, tc in enumerate(tcs):
                    hb = h2[i % 2]
                    w1_toks = {jj: load_pair(jj) for jj in range(3)}
                    h2_tok = {}
                    for j in range(NPAIR):
                        par = j % 3
                        lmm = [None, None]
                        for half in range(2):
                            bank = pb[2 * (j % 2) + half]
                            for ci in range(NKT):
                                lmm[half] = kb.op("tensor", lambda e, j=j, par=par, half=half, ci=ci, bank=bank: e.matmul(
                                    bank[:, :],
                                    w1p[par][:, ci * 256 + half * 128: ci * 256 + half * 128 + 128],
                                    xnbf[ci][:, :],
                                    start=(ci == 0), stop=(ci == NKT - 1)),
                                    waits=([w1_toks[j], h2_tok.get(j - 2 if pipe[2] else j - 1), xn_toks[0]] +
                                           ([xn_toks[NKT - 1]] if j == 0 else [])) if ci == 0
                                    else [("soft", lmm[half]), xn_toks[ci]],
                                    inc=(sp if ci == NKT - 1 else None))
                        w1_free[par] = lmm[1]
                        if j + 3 < NPAIR:
                            w1_toks[j + 3] = load_pair(j + 3)
                        tg_ = kb.op("scalar", lambda e, j=j, l=l: e.activation(
                            out=gel[j % 2][:, :], in_=pb[2 * (j % 2) + 1][:, :],
                            func=AF.Gelu, bias=lpc(l, 17 + 2 * j)),
                            waits=[lmm[1], h2_tok.get(j - 2 if pipe[2] else j - 1)], inc=sa)
                        h2_tok[j] = kb.op("vector", lambda e, j=j, l=l, hb=hb: e.scalar_tensor_tensor(
                            out=hb[:, j * 512:(j + 1) * 512], in0=pb[2 * (j % 2)][:, :],
                            scalar=lpc(l, 16 + 2 * j), in1=gel[j % 2][:, :],
                            op0=ALU.add, op1=ALU.mult),
                            waits=[lmm[0], tg_, h2_guard[i % 2]], inc=sv)
                    w1_done = lmm[1]
                    out_toks = []
                    for kt in range(NKT):
                        bank = pb[4 + kt % 4]
                        lastmm = None
                        for j in range(NPAIR):
                            lastmm = kb.op("tensor", lambda e, j=j, kt=kt, bank=bank, hb=hb: e.matmul(
                                bank[:, :],
                                w2[:, j * DIM + kt * 128: j * DIM + (kt + 1) * 128],
                                hb[:, j * 512:(j + 1) * 512],
                                start=(j == 0), stop=(j == NPAIR - 1)),
                                waits=([h2_tok[j], tw2_c[j]] + evac_hist[-4:]) if j == 0 else [h2_tok[j], tw2_c[j], ("soft", lastmm)],
                                inc=(sp if j == NPAIR - 1 else None))
                        out_toks.append(kb.op("vector", lambda e, kt=kt, tc=tc, l=l, bank=bank: e.scalar_tensor_tensor(
                            out=xT[kt][:, 1 + tc * 512: 1 + (tc + 1) * 512],
                            in0=bank[:, :], scalar=lpc(l, 60 + kt),
                            in1=xT[kt][:, 1 + tc * 512: 1 + (tc + 1) * 512],
                            op0=ALU.add, op1=ALU.add), waits=[lastmm], inc=sv))
                        evac_hist.append(out_toks[-1])
                    h2_guard[i % 2] = out_toks[-1]
                    if i + 1 < len(tcs):
                        stats_free = h2_tok[NPAIR - 1]
                        bcast_free = [h2_tok[NPAIR - 2], h2_tok[NPAIR - 2]]
                        xn_frees = [h2_tok[NPAIR - 1], h2_tok[NPAIR - 2]]
                        xn_toks = stats_apply(tcs[i + 1])
                    prev_out = out_toks[-1]
                if f"L{l}f" in dbg:
                    dump_x(s, f"L{l}f", prev_out)

        def run_final(s):
            kb.barrier()
            with ExitStack() as p3:
                sb3 = lambda n_, sh, d_: p3.enter_context(nc.sbuf_tensor(_nm(n_), sh, d_))
                gtok = sb3("gtok", [128, DIM], F32)
                stok = [sb3(f"stok{i}", [128, DIM], F32) for i in range(2)]
                x0tok = [sb3(f"x0tok{i}", [128, DIM], F32) for i in range(2)]
                otok = [sb3(f"otok{i}", [128, DIM], F32) for i in range(2)]
                prev = [None, None]
                prev_g = None
                prev_p3 = {}
                for tc in range(NT128):
                    par = tc % 2
                    kb.op("sync", lambda e, tc=tc, par=par: e.dma_start(
                        stok[par][:, :], d_sscr[s, tc]), waits=[prev[par]],
                        inc=sd_par[par], by=16)
                    tlx = kb.op("sync", lambda e, tc=tc, par=par: e.dma_start(
                        x0tok[par][:, :], d_x[s, tc * 128:(tc + 1) * 128, :]),
                        waits=[prev[par]], inc=sd_par[par], by=16)
                    tls = tlx
                    gevs = {}
                    for k in range(NKT):
                        bank = pb[4 + (k % 2)]
                        tp = kb.op("tensor", lambda e, k=k, tc=tc, bank=bank: e.transpose(
                            bank[:, 0:128],
                            xT[k][:, 1 + tc * 128: 1 + (tc + 1) * 128],
                            identf[:, :]),
                            waits=[gevs.get(k - 2), prev_g, prev_p3.get(k % 2)],
                            inc=sp)
                        gevs[k] = kb.op("scalar", lambda e, k=k, bank=bank: e.activation(
                            out=gtok[:, k * 128:(k + 1) * 128], in_=bank[:, 0:128],
                            func=AF.Silu), waits=[tp], inc=sa)
                    prev_p3[0] = gevs[NKT - 2]
                    prev_p3[1] = gevs[NKT - 1]
                    t1 = kb.op("vector", lambda e, par=par: e.tensor_mul(
                        otok[par][:, :], stok[par][:, :], gtok[:, :]),
                        waits=[gevs[NKT - 1], tls], inc=sv)
                    prev_g = t1
                    t2 = kb.op("vector", lambda e, par=par: e.tensor_add(
                        otok[par][:, :], otok[par][:, :], x0tok[par][:, :]),
                        waits=[t1, tlx], inc=sv)
                    prev[par] = kb.op("sync", lambda e, tc=tc, par=par: e.dma_start(
                        d_out[s, tc * 128:(tc + 1) * 128, :], otok[par][:, :]),
                        waits=[t2], inc=st_par[par], by=16)

        def dump_x(s, tag, wait_tok):
            if s != 0:
                return
            for k in range(NKT):
                kb.op("sync", lambda e, k=k, tag=tag: e.dma_start(
                    d_dbg[tag][k], xT[k][:, 1:T + 1]), waits=[wait_tok],
                    inc=st_, by=16)

        for s in range(NSEQ):
            run_p0(s)
            run_sbranch(s)
            for l in range(DEPTH):
                run_attn(s, l)
                run_ffn(s, l)
            if "xt" in dbg:
                kb.barrier()
                dump_x(s, "xt", None)
            run_final(s)

        kb.op("sync", None, waits=[(st_, st_.n), (st_par[0], st_par[0].n),
                                   (st_par[1], st_par[1].n)])
        kb.emit()
    return nc


def pack_inputs(ins, T, NSEQ, DEPTH, consts):
    m = {}
    m["decay_col"] = np.full((128, 1), np.float32(ins["decay"]), np.float32)
    m["wT"] = np.ascontiguousarray(ins["W"].T).astype(np.float32)
    m["grep"] = np.broadcast_to(ins["gamma"], (128, DIM)).astype(np.float32).copy()
    m["brep"] = np.broadcast_to(ins["beta"], (128, DIM)).astype(np.float32).copy()
    m["yrep"] = np.broadcast_to(ins["b"], (128, DIM)).astype(np.float32).copy()
    m["wqkv"] = np.ascontiguousarray(ins["Wqkv"]).astype(np.float32)
    m["wo"] = np.ascontiguousarray(ins["Wo"]).astype(np.float32)
    m["w1"] = np.ascontiguousarray(ins["W1"]).astype(np.float32)
    m["w2"] = np.ascontiguousarray(ins["W2"]).astype(np.float32)
    lpv = np.zeros((DEPTH, 128, 70), np.float32)
    for l in range(DEPTH):
        lpv[l, :, 0:8] = ins["ln_g"][l].reshape(8, 128).T
        lpv[l, :, 8:16] = ins["ln_b"][l].reshape(8, 128).T
        b1 = ins["b1"][l]
        for j in range(NPAIR):
            n = min(128, DI - j * 128)
            lpv[l, 0:n, 16 + 2 * j] = b1[j * 128: j * 128 + n]
            lpv[l, 0:n, 17 + 2 * j] = b1[DI + j * 128: DI + j * 128 + n]
        lpv[l, :, 60:68] = ins["b2"][l].reshape(8, 128).T
        lpv[l, :, 68] = np.concatenate([ins["q_scale"][l]] * 2)
        lpv[l, :, 69] = np.concatenate([ins["k_scale"][l]] * 2)
    m["lp"] = lpv
    for k, v in consts.items():
        m[k] = v
    return m


_CACHE = {}


def kernel(L0_attn_Z, L0_attn_decay, L0_attn_W, L0_attn_b, L0_attn_gamma,
           L0_attn_beta, L0_ffn_Wqkv, L0_ffn_q_scale, L0_ffn_k_scale,
           L0_ffn_Wo, L0_ffn_ln_g, L0_ffn_ln_b, L0_ffn_W1, L0_ffn_b1,
           L0_ffn_W2, L0_ffn_b2):
    x = np.asarray(L0_attn_Z)
    B, T, D = x.shape
    NSEQ = B // 8
    DEPTH = 6
    ins = dict(decay=np.float32(L0_attn_decay), W=np.asarray(L0_attn_W),
               b=np.asarray(L0_attn_b), gamma=np.asarray(L0_attn_gamma),
               beta=np.asarray(L0_attn_beta), Wqkv=np.asarray(L0_ffn_Wqkv),
               q_scale=np.asarray(L0_ffn_q_scale),
               k_scale=np.asarray(L0_ffn_k_scale), Wo=np.asarray(L0_ffn_Wo),
               ln_g=np.asarray(L0_ffn_ln_g), ln_b=np.asarray(L0_ffn_ln_b),
               W1=np.asarray(L0_ffn_W1), b1=np.asarray(L0_ffn_b1),
               W2=np.asarray(L0_ffn_W2), b2=np.asarray(L0_ffn_b2))
    key = (T, NSEQ, DEPTH)
    if key not in _CACHE:
        _CACHE[key] = build_program(T=T, NSEQ=NSEQ, DEPTH=DEPTH)
    nc = _CACHE[key]
    consts = make_consts(T)
    common = pack_inputs(ins, T, NSEQ, DEPTH, consts)
    in_maps = []
    for c in range(8):
        im = dict(common)
        im["x"] = np.ascontiguousarray(x[c * NSEQ:(c + 1) * NSEQ])
        in_maps.append(im)
    res = run_bass_kernel_spmd(nc, in_maps, core_ids=list(range(8)))
    out = np.concatenate([r["out"] for r in res.results], axis=0)
    return out.astype(np.float32)


# version 15 (final)
# speedup vs baseline: 1.0958x; 1.0958x over previous
import numpy as np
import ml_dtypes
from contextlib import ExitStack

import concourse.bass as bass
import concourse.mybir as mybir
from concourse.bass_utils import run_bass_kernel_spmd

F32 = mybir.dt.float32
BF16 = mybir.dt.bfloat16
AF = mybir.ActivationFunctionType
ALU = mybir.AluOpType
AX = mybir.AxisListType

DIM = 1024
NKT = 8
WIN = 256
DI = 2730
NPAIR = 22
LN_EPS = 1e-5
QK_SCALE = 8.0
NEG = -1.0e30

ENGINES = ("sync", "tensor", "vector", "scalar", "gpsimd")


class Sem:
    __slots__ = ("h", "n", "name")

    def __init__(self, h, name):
        self.h, self.n, self.name = h, 0, name


class OpRef:
    __slots__ = ("i", "eng", "fn", "deps", "dma", "cost", "pos", "sem_val",
                 "finish", "nodep", "softs")

    def __init__(self, i, eng, fn, deps, dma, cost):
        self.i, self.eng, self.fn = i, eng, fn
        self.deps, self.dma, self.cost = deps, dma, cost
        self.pos = -1
        self.sem_val = None
        self.finish = 0.0
        self.nodep = True


def _default_cost(eng):
    return {"tensor": 250.0, "vector": 450.0, "scalar": 400.0,
            "gpsimd": 900.0, "sync": 300.0}[eng]


class KB:

    def __init__(self, nc, ctx):
        self.nc = nc
        self.ctx = ctx
        self.ops = []
        self.all_sems = []
        self.esem = {e: None for e in ENGINES}
        self.last_barrier = {e: None for e in ENGINES}
        self.last_op = {e: None for e in ENGINES}
        self.dma_last = {}
        self.dma_issues = {}
        self.compute_sems = set()

    def init_esems(self):
        for e in ENGINES:
            self.esem[e] = self.sem("es_" + e)

    def sem(self, name):
        s = Sem(self.ctx.enter_context(self.nc.semaphore(name)), name)
        self.all_sems.append(s)
        return s

    def op(self, eng, fn, waits=(), inc=None, by=16, cost=None):
        if inc is not None and inc in self.compute_sems:
            inc = None
        deps = []
        softs = []
        for tok in waits:
            if tok is None:
                continue
            if isinstance(tok, tuple) and len(tok) == 2 and tok[0] == "soft":
                if tok[1] is not None:
                    softs.append(tok[1])
                continue
            deps.append(tok)
            if not isinstance(tok, OpRef):
                sem_, val_ = tok
                lst = self.dma_issues.get(id(sem_))
                if lst is not None and val_ > 0:
                    import bisect
                    k_ = bisect.bisect_left(lst, (val_,), key=lambda x: (x[0],))
                    if k_ < len(lst):
                        softs.append(lst[k_][1])
        if self.last_barrier[eng] is not None:
            deps.append(self.last_barrier[eng])
        dma = None
        if inc is not None:
            inc.n += by
            dma = (inc, inc.n, by)
            if inc in self.dma_last:
                deps.append(self.dma_last[inc])
        o = OpRef(len(self.ops), eng, fn, deps, dma,
                  cost if cost is not None else _default_cost(eng))
        o.softs = softs
        for d in deps:
            if isinstance(d, OpRef):
                d.nodep = False
        self.ops.append(o)
        self.last_op[eng] = o
        if inc is not None:
            self.dma_last[inc] = o
            self.dma_issues.setdefault(id(inc), []).append((inc.n, o))
            return (inc, inc.n)
        return o

    def barrier(self):
        deps = [o for o in self.last_op.values() if o is not None]
        softs = []
        for s_ in self.all_sems:
            if s_.n > 0 and not s_.name.startswith("es_"):
                deps.append((s_, s_.n))
                lst = self.dma_issues.get(id(s_))
                if lst:
                    softs.append(lst[-1][1])
        for eng in ENGINES:
            b = OpRef(len(self.ops), eng, None, list(deps), None, 50.0)
            b.softs = list(softs)
            for d in deps:
                if isinstance(d, OpRef):
                    d.nodep = False
            self.ops.append(b)
            self.last_barrier[eng] = b
            self.last_op[eng] = b

    def emit(self):
        import heapq
        n = len(self.ops)
        ndep = [0] * n
        children = [[] for _ in range(n)]
        for o in self.ops:
            for d in o.deps:
                if isinstance(d, OpRef):
                    ndep[o.i] += 1
                    children[d.i].append(o.i)
            for d in o.softs:
                if isinstance(d, OpRef):
                    ndep[o.i] += 1
                    children[d.i].append(o.i)
        clock = {e: 0.0 for e in ENGINES}
        ready = {e: [] for e in ENGINES}
        est = [0.0] * n
        for o in self.ops:
            if ndep[o.i] == 0:
                heapq.heappush(ready[o.eng], (0.0, o.i))
        order = {e: [] for e in ENGINES}
        scheduled = 0
        pending = {e: [] for e in ENGINES}
        while scheduled < n:
            progressed = False
            for e in ENGINES:
                if not ready[e]:
                    continue
                er, i = heapq.heappop(ready[e])
                o = self.ops[i]
                start = max(clock[e], er)
                o.finish = start + o.cost
                clock[e] = o.finish
                o.pos = len(order[e])
                order[e].append(o)
                scheduled += 1
                progressed = True
                for c in children[i]:
                    ndep[c] -= 1
                    est[c] = max(est[c], o.finish)
                    if ndep[c] == 0:
                        heapq.heappush(ready[self.ops[c].eng], (est[c], c))
            if not progressed:
                raise RuntimeError("scheduler deadlock: cyclic deps")
        for e in ENGINES:
            cnt = 0
            for o in order[e]:
                if o.dma is None and not o.nodep:
                    cnt += 1
                    o.sem_val = cnt
            self.esem[e].n = cnt
        def make_thunk(o):
            wl = {}
            for d in o.deps:
                if isinstance(d, OpRef):
                    if d.dma is not None:
                        h, v = d.dma[0].h, d.dma[1]
                    else:
                        assert d.sem_val is not None, (d.i, d.eng, "no sem")
                        h, v = self.esem[d.eng].h, d.sem_val
                else:
                    h, v = d[0].h, d[1]
                    if v <= 0:
                        continue
                key = id(h)
                wl[key] = (h, max(wl.get(key, (None, 0))[1], v))
            wl = list(wl.values())
            inc_h = None
            inc_by = 0
            if o.dma is not None:
                inc_h, inc_by = o.dma[0].h, o.dma[2]
            elif o.sem_val is not None:
                inc_h, inc_by = self.esem[o.eng].h, 1

            def thunk(e, o=o, wl=wl, inc_h=inc_h, inc_by=inc_by):
                for h, v in wl:
                    e.wait_ge(h, v)
                if o.fn is not None:
                    r = o.fn(e)
                    if inc_h is not None:
                        r.then_inc(inc_h, inc_by)
                elif inc_h is not None:
                    e.nop().then_inc(inc_h, inc_by)
            return thunk
        blk = self.ctx.enter_context(self.nc.Block())
        for eng in ENGINES:
            lst = [make_thunk(o) for o in order[eng]]
            if lst:
                getattr(blk, eng)(lambda e, lst=lst: [f(e) for f in lst])


def make_consts(T):
    DH = 64
    c = {}
    c["ident_f"] = np.eye(128, dtype=np.float32)
    c["ident_b"] = np.eye(128, dtype=ml_dtypes.bfloat16)
    pm = np.zeros((128, 128), np.float32)
    for g in (0, 64):
        for a in range(32):
            pm[g + a + 32, g + a] = -1.0
            pm[g + a, g + a + 32] = 1.0
    c["perm_b"] = pm.astype(ml_dtypes.bfloat16)

    half = np.arange(0, DH, 2, dtype=np.float64)
    inv_freq = 1.0 / 10000.0 ** (half / DH)
    t = np.arange(T, dtype=np.float64)
    fr = t[:, None] * inv_freq[None, :]
    fr = np.concatenate([fr, fr], -1)
    base = (half + 0.4 * DH) / (1.4 * DH)
    power = (t - T // 2) / (WIN // 2)
    sc = np.concatenate([base[None, :] ** power[:, None]] * 2, -1)
    cos, sin = np.cos(fr), np.sin(fr)
    tabs = np.stack([cos * sc * QK_SCALE, sin * sc * QK_SCALE, cos / sc, sin / sc])
    tabs = np.transpose(tabs.astype(np.float32), (0, 2, 1))
    c["tabs"] = np.ascontiguousarray(
        tabs.reshape(2, 128, tabs.shape[2])).astype(np.float32)

    p = np.arange(128)[:, None]
    cg = np.arange(384)[None, :]
    c["msk_g"] = np.where((cg >= p) & (cg <= p + 256), 0.0, NEG).astype(ml_dtypes.bfloat16)
    c128 = np.arange(128)[None, :]
    tri = np.where(c128 <= p, 0.0, NEG).astype(np.float32)
    c["msk_w0a"] = tri.astype(ml_dtypes.bfloat16)
    c["msk_w0b"] = np.concatenate([np.zeros((128, 128), np.float32), tri],
                                  axis=1).astype(ml_dtypes.bfloat16)

    eq2 = np.zeros((128, 2), np.float32)
    eq2[0:64, 0] = 1.0
    eq2[64:128, 1] = 1.0
    c["eq2"] = eq2.astype(ml_dtypes.bfloat16)
    e2f = np.zeros((2, 128), np.float32)
    e2f[0, 0:64] = 1.0
    e2f[1, 64:128] = 1.0
    c["e2f"] = e2f

    s1 = np.zeros((128, 33), np.float32)
    s1[:, 0] = 1.0 / 1024.0
    c["st1"] = s1.astype(ml_dtypes.bfloat16)
    s33 = np.zeros((128, 33), np.float32)
    s33[:, 32] = 1.0 / 1024.0
    c["st33"] = s33.astype(ml_dtypes.bfloat16)
    c["ones_row_f"] = np.ones((1, 128), np.float32)
    return c


def build_program(T=2048, NSEQ=2, DEPTH=6, dbg=(), pipe=(1, 1, 1)):
    nc = bass.Bass("TRN2", target_bir_lowering=False)
    NW = T // WIN
    NT128 = T // 128
    NT512 = T // 512

    dt = nc.dram_tensor
    d_x = dt("x", [NSEQ, T, DIM], F32, kind="ExternalInput").ap()
    d_decay = dt("decay_col", [128, 1], F32, kind="ExternalInput").ap()
    d_wT = dt("wT", [DIM, DIM], F32, kind="ExternalInput").ap()
    d_grep = dt("grep", [128, DIM], F32, kind="ExternalInput").ap()
    d_brep = dt("brep", [128, DIM], F32, kind="ExternalInput").ap()
    d_yrep = dt("yrep", [128, DIM], F32, kind="ExternalInput").ap()
    d_wqkv = dt("wqkv", [DEPTH, DIM, 3 * DIM], F32, kind="ExternalInput").ap()
    d_wo = dt("wo", [DEPTH, DIM, DIM], F32, kind="ExternalInput").ap()
    d_w1 = dt("w1", [DEPTH, NPAIR, 128, NKT * 256], F32, kind="ExternalInput").ap()
    d_w2 = dt("w2", [DEPTH, 128, NPAIR * DIM], F32, kind="ExternalInput").ap()
    d_w1b = dt("w1b", [DEPTH, NPAIR, 128, NKT * 256], BF16, kind="Internal").ap()
    d_w2b = dt("w2b", [DEPTH, 128, NPAIR * DIM], BF16, kind="Internal").ap()
    d_lp = dt("lp", [DEPTH, 128, 70], F32, kind="ExternalInput").ap()
    d_identf = dt("ident_f", [128, 128], F32, kind="ExternalInput").ap()
    d_identb = dt("ident_b", [128, 128], BF16, kind="ExternalInput").ap()
    d_perm = dt("perm_b", [128, 128], BF16, kind="ExternalInput").ap()
    d_tabs = dt("tabs", [2, 128, T], F32, kind="ExternalInput").ap()
    d_mskg = dt("msk_g", [128, 384], BF16, kind="ExternalInput").ap()
    d_mskw0a = dt("msk_w0a", [128, 128], BF16, kind="ExternalInput").ap()
    d_mskw0b = dt("msk_w0b", [128, 256], BF16, kind="ExternalInput").ap()
    d_eq2 = dt("eq2", [128, 2], BF16, kind="ExternalInput").ap()
    d_e2f = dt("e2f", [2, 128], F32, kind="ExternalInput").ap()
    d_st1 = dt("st1", [128, 33], BF16, kind="ExternalInput").ap()
    d_st33 = dt("st33", [128, 33], BF16, kind="ExternalInput").ap()
    d_onesrow = dt("ones_row_f", [1, 128], F32, kind="ExternalInput").ap()
    d_sscr = dt("sscr", [NSEQ, NT128, 128, DIM], F32,
                kind=("ExternalOutput" if "sscr" in dbg else "Internal")).ap()
    d_out = dt("out", [NSEQ, T, DIM], F32, kind="ExternalOutput").ap()
    d_dbg = {}
    for tag in (t for t in dbg if t != "sscr" and not t.startswith("aw")):
        d_dbg[tag] = dt("dbg_" + tag, [NKT, 128, T], F32, kind="ExternalOutput").ap()
    WDBG = None
    for tag in dbg:
        if tag.startswith("aw"):
            WDBG = int(tag[2:])
            d_dbg["qrot"] = dt("dbg_qrot", [NKT, 128, WIN], F32, kind="ExternalOutput").ap()
            d_dbg["krot"] = dt("dbg_krot", [NKT, 128, 2 * WIN], F32, kind="ExternalOutput").ap()
            d_dbg["vtok"] = dt("dbg_vtok", [4, 128, DIM], F32, kind="ExternalOutput").ap()
            d_dbg["aout"] = dt("dbg_aout", [NKT, 128, WIN], F32, kind="ExternalOutput").ap()

    ctx = ExitStack()
    kb = KB(nc, ctx)
    _uid = [0]

    def _nm(name):
        _uid[0] += 1
        return f"{name}_u{_uid[0]}"
    sbp = lambda name, shape, dtype: ctx.enter_context(nc.sbuf_tensor(name, shape, dtype))

    with ctx:
        xT = [sbp(f"xT{k}", [128, T + 1], F32) for k in range(NKT)]
        tab2 = [sbp(f"tab2_{i}", [128, T], F32) for i in range(2)]
        tabsrc = [(tab2[0], 0), (tab2[0], 64), (tab2[1], 0), (tab2[1], 64)]
        mskg = sbp("mskg", [128, 384], BF16)
        mskw0a = sbp("mskw0a", [128, 128], BF16)
        mskw0b = sbp("mskw0b", [128, 256], BF16)
        identf = sbp("identf", [128, 128], F32)
        identb = sbp("identb", [128, 128], BF16)
        perm = sbp("perm", [128, 128], BF16)
        eq2 = sbp("eq2c", [128, 2], BF16)
        e2f = sbp("e2fc", [2, 128], F32)
        eps24 = sbp("eps24", [128, 1], F32)
        st1 = sbp("st1c", [128, 33], BF16)
        st33 = sbp("st33c", [128, 33], BF16)
        onesrow = sbp("onesrow", [1, 128], F32)
        lp = sbp("lp_sb", [128, 70 * max(DEPTH, 1)], F32)
        decay = sbp("decay", [128, 1], F32)
        epscol = sbp("epscol", [128, 1], F32)
        acol = sbp("acol", [128, 1], F32)
        atile = sbp("atile", [128, 512], F32)
        pb = [ctx.enter_context(nc.psum_tensor(f"pb{i}", [128, 512], F32))
              for i in range(8)]

        kb.init_esems()
        sd_c = kb.sem("sd_c")
        sd_x = kb.sem("sd_x")
        sd_w = kb.sem("sd_w")
        sv = kb.sem("sv")
        sg = kb.sem("sg")

        sa = kb.sem("sa")
        sp = kb.sem("sp")
        st_ = kb.sem("st")
        kb.compute_sems = {sv, sa, sp, sg}
        w1s = [kb.sem("w1s0"), kb.sem("w1s1")]
        sd_par = [kb.sem("sd_par0"), kb.sem("sd_par1")]
        sd_wq = kb.sem("sd_wq")
        sd_wo2 = kb.sem("sd_wo2")
        sd_w2s = kb.sem("sd_w2s")
        sd_wt = kb.sem("sd_wt")
        st_par = [kb.sem("st_par0"), kb.sem("st_par1")]
        wqg = [kb.sem(f"wqg{i}") for i in range(4)]
        w2g = [kb.sem(f"w2g{i}") for i in range(5)]
        w1se = {"sync": kb.sem("w1se_y"), "scalar": kb.sem("w1se_a"),
                "gpsimd": kb.sem("w1se_g")}
        w2gs = [kb.sem(f"w2gs{i}") for i in range(4)]
        swb = kb.sem("swb")
        wb1_tok = {}
        wb2_tok = {}

        for t_sb, t_d in [(identf, d_identf), (identb, d_identb), (perm, d_perm),
                          (mskg, d_mskg), (mskw0a, d_mskw0a), (mskw0b, d_mskw0b),
                          (eq2, d_eq2), (e2f, d_e2f), (st1, d_st1), (st33, d_st33),
                          (onesrow, d_onesrow), (decay, d_decay)]:
            kb.op("sync", lambda e, o=t_sb, i=t_d: e.dma_start(o[:, :], i),
                  inc=sd_c, by=16)
        for i in range(2):
            kb.op("sync", lambda e, o=tab2[i], i_=d_tabs[i]: e.dma_start(o[:, :], i_),
                  inc=sd_c, by=16)
        kb.op("sync", lambda e: e.dma_start(
            lp.ap().rearrange("p (l c) -> p l c", l=DEPTH),
            d_lp.rearrange("l p c -> p l c")), inc=sd_c, by=16)
        CT = (sd_c, sd_c.n)

        def lpc(l, col):
            return lp[:, 70 * l + col: 70 * l + col + 1]

        for k in range(NKT):
            kb.op("vector", lambda e, k=k: e.memset(xT[k][:, 0:1], 0.0), inc=sv)
        t_sig = kb.op("scalar", lambda e: e.activation(
            out=acol[:, 0:1], in_=decay[:, 0:1], func=AF.Sigmoid),
            waits=[CT], inc=sa)
        teps = kb.op("vector", lambda e: e.memset(epscol[:, :], LN_EPS), inc=sv)
        teps24 = kb.op("vector", lambda e: e.memset(eps24[:, :], 1e-24), inc=sv)
        t_at0 = kb.op("vector", lambda e: e.memset(atile[:, :], 1.0), inc=sv)
        t_at = kb.op("vector", lambda e: e.tensor_scalar_mul(
            atile[:, :], atile[:, :], acol[:, 0:1]), waits=[t_sig, t_at0], inc=sv)

        def run_p0(s):
            kb.barrier()
            with ExitStack() as p0:
                sb0 = lambda n_, sh, d_: p0.enter_context(nc.sbuf_tensor(_nm(n_), sh, d_))
                xtok = [sb0(f"p0x{i}", [128, DIM], F32) for i in range(2)]
                xsq = sb0("p0sq", [128, DIM], F32)
                cmu = sb0("p0mu", [128, 1], F32)
                cm2 = sb0("p0m2", [128, 1], F32)
                cvar = sb0("p0var", [128, 1], F32)
                crs = sb0("p0rs", [128, 1], F32)
                xn = sb0("p0xn", [128, DIM], F32)
                prev_done = [None, None]
                prev_txn = None
                prev_tp = None
                prev_p0ev = {}
                for tc in range(NT128):
                    par = tc % 2
                    tl = kb.op("sync", lambda e, tc=tc, par=par: e.dma_start(
                        xtok[par][:, :], d_x[s, tc * 128:(tc + 1) * 128, :]),
                        waits=[prev_done[par]], inc=sd_par[par], by=16)
                    t1 = kb.op("vector", lambda e, par=par: e.tensor_mul(
                        xsq[:, :], xtok[par][:, :], xtok[par][:, :]),
                        waits=[tl, prev_txn], inc=sv)
                    t2 = kb.op("vector", lambda e, par=par: e.reduce_sum(
                        cmu[:, 0:1], xtok[par][:, :], axis=AX.X),
                        waits=[tl, prev_txn], inc=sv)
                    t3 = kb.op("vector", lambda e: e.reduce_sum(
                        cm2[:, 0:1], xsq[:, :], axis=AX.X), waits=[t1], inc=sv)
                    t4 = kb.op("vector", lambda e: e.tensor_scalar_mul(
                        cmu[:, 0:1], cmu[:, 0:1], 1.0 / DIM), waits=[t2], inc=sv)
                    t5 = kb.op("vector", lambda e: e.tensor_scalar(
                        out=cvar[:, 0:1], in0=cmu[:, 0:1], scalar1=cmu[:, 0:1],
                        scalar2=-1.0, op0=ALU.mult, op1=ALU.mult), waits=[t4], inc=sv)
                    t6 = kb.op("vector", lambda e: e.tensor_scalar(
                        out=cvar[:, 0:1], in0=cm2[:, 0:1], scalar1=1.0 / DIM,
                        scalar2=cvar[:, 0:1], op0=ALU.mult, op1=ALU.add),
                        waits=[t3, t5], inc=sv)
                    t7 = kb.op("scalar", lambda e: e.activation(
                        out=crs[:, 0:1], in_=cvar[:, 0:1], func=AF.Ln,
                        bias=epscol[:, 0:1]), waits=[t6, teps], inc=sa)
                    t8 = kb.op("scalar", lambda e: e.activation(
                        out=crs[:, 0:1], in_=crs[:, 0:1], func=AF.Exp, scale=-0.5),
                        waits=[t7], inc=sa)
                    tneg = kb.op("vector", lambda e: e.tensor_scalar_mul(
                        cmu[:, 0:1], cmu[:, 0:1], -1.0), waits=[t5], inc=sv)
                    txn = kb.op("vector", lambda e, par=par: e.tensor_scalar(
                        out=xn[:, :], in0=xtok[par][:, :], scalar1=cmu[:, 0:1],
                        scalar2=crs[:, 0:1], op0=ALU.add, op1=ALU.mult),
                        waits=[t8, tneg, prev_tp], inc=sv)
                    evs = {}
                    for k in range(NKT):
                        bank = pb[k % 2]
                        tp = kb.op("tensor", lambda e, k=k, bank=bank: e.transpose(
                            bank[:, 0:128], xn[:, k * 128:(k + 1) * 128], identf[:, :]),
                            waits=[txn, evs.get(k - 2), prev_p0ev.get(par)], inc=sp)
                        evs[k] = kb.op("scalar", lambda e, k=k, tc=tc, bank=bank: e.copy(
                            xT[k][:, 1 + tc * 128: 1 + (tc + 1) * 128], bank[:, 0:128]),
                            waits=[tp], inc=sa)
                        prev_tp = tp
                    prev_p0ev[par] = evs[NKT - 1]
                    prev_done[par] = evs[NKT - 1]
                    prev_txn = txn

        def run_sbranch(s):
            kb.barrier()
            with ExitStack() as p1:
                sb1 = lambda n_, sh, d_: p1.enter_context(nc.sbuf_tensor(_nm(n_), sh, d_))
                S_bf = [sb1(f"Sbf{k}", [128, T], BF16) for k in range(NKT)]
                scanf = [sb1(f"scanf{p}", [128, T], F32) for p in range(2)]
                wT_bf = sb1("wTbf", [128, NKT * DIM], BF16)
                grep = sb1("grep_s", [128, DIM], F32)
                brep = sb1("brep_s", [128, DIM], F32)
                yrep = sb1("yrep_s", [128, DIM], F32)
                yf = sb1("yf", [128, DIM], F32)
                ysq = sb1("ysq", [128, DIM], F32)
                sout = sb1("sout", [128, DIM], F32)
                cmu = sb1("s_mu", [128, 1], F32)
                cm2 = sb1("s_m2", [128, 1], F32)
                cvar = sb1("s_var", [128, 1], F32)
                crs = sb1("s_rs", [128, 1], F32)
                if s == 0:
                    print("SBUF watermark (sbranch):", nc.sbuf_base)

                tw = kb.op("gpsimd", lambda e: e.dma_start(
                    wT_bf.ap().rearrange("p (kc e) -> p kc e", kc=NKT),
                    d_wT.rearrange("(kc p) e -> p kc e", p=128)),
                    inc=sd_wt, by=16)
                kb.op("sync", lambda e: e.dma_start(grep[:, :], d_grep),
                      inc=sd_x, by=16)
                kb.op("sync", lambda e: e.dma_start(brep[:, :], d_brep),
                      inc=sd_x, by=16)
                trep = kb.op("sync", lambda e: e.dma_start(yrep[:, :], d_yrep),
                             inc=sd_x, by=16)
                tg = tb = ty = trep

                sbf_toks = [None] * NKT
                last_k = {}
                for k in range(NKT):
                    p = k % 2
                    tprev = last_k.get(k - 2, t_at)
                    for ch in range(T // 512):
                        init = 0.0 if ch == 0 else scanf[p][:, ch * 512 - 1: ch * 512]
                        tprev = kb.op("vector", lambda e, k=k, ch=ch, init=init, p=p: e.tensor_tensor_scan(
                            out=scanf[p][:, ch * 512:(ch + 1) * 512],
                            data0=atile[:, :],
                            data1=xT[k][:, 1 + ch * 512: 1 + (ch + 1) * 512],
                            initial=init, op0=ALU.mult, op1=ALU.add),
                            waits=[tprev, t_at], inc=sv)
                    tprev = kb.op("vector", lambda e, k=k, p=p: e.tensor_copy(
                        S_bf[k][:, :], scanf[p][:, :]), waits=[tprev], inc=sv)
                    last_k[k] = tprev
                    sbf_toks[k] = tprev

                bank_free = [None, None]
                yf_free = None
                prev_tsp = None
                for tc in range(NT128):
                    ytoks = []
                    for eh in range(2):
                        bank = pb[2 + eh]
                        lastmm = None
                        for k in range(NKT):
                            lastmm = kb.op("tensor", lambda e, k=k, eh=eh, tc=tc, bank=bank: e.matmul(
                                bank[:, :],
                                S_bf[k][:, tc * 128:(tc + 1) * 128],
                                wT_bf[:, k * DIM + eh * 512: k * DIM + (eh + 1) * 512],
                                start=(k == 0), stop=(k == NKT - 1)),
                                waits=[sbf_toks[k], tw, bank_free[eh]] if k == 0
                                else [sbf_toks[k], ("soft", lastmm)],
                                inc=(sp if k == NKT - 1 else None))
                        tev = kb.op("vector", lambda e, eh=eh, bank=bank: e.scalar_tensor_tensor(
                            out=yf[:, eh * 512:(eh + 1) * 512], in0=bank[:, :],
                            scalar=1.0, in1=yrep[:, eh * 512:(eh + 1) * 512],
                            op0=ALU.mult, op1=ALU.add),
                            waits=[lastmm, ty, yf_free], inc=sv)
                        bank_free[eh] = tev
                        ytoks.append(tev)
                    t1 = kb.op("vector", lambda e: e.tensor_mul(
                        ysq[:, :], yf[:, :], yf[:, :]), waits=ytoks, inc=sv)
                    t2 = kb.op("vector", lambda e: e.reduce_sum(
                        cmu[:, 0:1], yf[:, :], axis=AX.X), waits=ytoks, inc=sv)
                    t3 = kb.op("vector", lambda e: e.reduce_sum(
                        cm2[:, 0:1], ysq[:, :], axis=AX.X), waits=[t1], inc=sv)
                    t4 = kb.op("vector", lambda e: e.tensor_scalar_mul(
                        cmu[:, 0:1], cmu[:, 0:1], 1.0 / DIM), waits=[t2], inc=sv)
                    t5 = kb.op("vector", lambda e: e.tensor_scalar(
                        out=cvar[:, 0:1], in0=cmu[:, 0:1], scalar1=cmu[:, 0:1],
                        scalar2=-1.0, op0=ALU.mult, op1=ALU.mult), waits=[t4], inc=sv)
                    t6 = kb.op("vector", lambda e: e.tensor_scalar(
                        out=cvar[:, 0:1], in0=cm2[:, 0:1], scalar1=1.0 / DIM,
                        scalar2=cvar[:, 0:1], op0=ALU.mult, op1=ALU.add),
                        waits=[t3, t5], inc=sv)
                    t7 = kb.op("scalar", lambda e: e.activation(
                        out=crs[:, 0:1], in_=cvar[:, 0:1], func=AF.Ln,
                        bias=epscol[:, 0:1]), waits=[t6, teps], inc=sa)
                    t8 = kb.op("scalar", lambda e: e.activation(
                        out=crs[:, 0:1], in_=crs[:, 0:1], func=AF.Exp, scale=-0.5),
                        waits=[t7], inc=sa)
                    tneg = kb.op("vector", lambda e: e.tensor_scalar_mul(
                        cmu[:, 0:1], cmu[:, 0:1], -1.0), waits=[t5], inc=sv)
                    ta = kb.op("vector", lambda e: e.tensor_scalar(
                        out=sout[:, :], in0=yf[:, :], scalar1=cmu[:, 0:1],
                        scalar2=crs[:, 0:1], op0=ALU.add, op1=ALU.mult),
                        waits=[t8, tneg, prev_tsp], inc=sv)
                    tb2 = kb.op("vector", lambda e: e.tensor_mul(
                        sout[:, :], sout[:, :], grep[:, :]), waits=[ta, tg], inc=sv)
                    tc2 = kb.op("vector", lambda e: e.tensor_add(
                        sout[:, :], sout[:, :], brep[:, :]), waits=[tb2, tb], inc=sv)
                    prev_tsp = kb.op("sync", lambda e, tc=tc: e.dma_start(
                        d_sscr[s, tc], sout[:, :]), waits=[tc2], inc=sd_x, by=16)
                    yf_free = ta

        def run_attn(s, l):
            kb.barrier()
            with ExitStack() as pa:
                sba = lambda n_, sh, d_: pa.enter_context(nc.sbuf_tensor(_nm(n_), sh, d_))
                wqkv = sba("wqkv", [128, NKT * 3 * DIM], BF16)
                wo = sba("wo_l", [128, NKT * DIM], BF16)
                xbfw = [sba(f"xbfw{k}", [128, WIN + 1], BF16) for k in range(NKT)]
                q_rot = [sba(f"qrot{k}", [128, WIN], BF16) for k in range(NKT)]
                k_buf = [sba(f"kbuf{k}", [128, 2 * WIN], BF16) for k in range(NKT)]
                v_buf = [sba(f"vbuf{j}", [128, DIM], BF16) for j in range(4)]
                qbfe = [sba(f"qbfe{p}", [128, WIN], BF16) for p in range(2)]
                q2b = [sba(f"q2b{p}", [128, WIN], BF16) for p in range(2)]
                fscr = [sba(f"fscr{p}", [128, 1152], F32) for p in range(2)]
                qf = [fscr[p][:, 0:WIN] for p in range(2)]
                rotA = [fscr[p][:, WIN:2 * WIN] for p in range(2)]
                rotB = [fscr[p][:, 2 * WIN:3 * WIN] for p in range(2)]
                msk_sc = [[fscr[0][:, 0:384], fscr[0][:, 384:768]],
                          [fscr[1][:, 0:384], fscr[1][:, 384:768]],
                          [fscr[0][:, 768:1152], fscr[1][:, 768:1152]]]
                ssr = [sba(f"ssr{p}", [2, WIN], F32) for p in range(2)]
                twt = [sba(f"twt{i}", [128, WIN], F32) for i in range(4)]
                Pbf = [[sba(f"Pbf{p}{ic}", [128, 384], BF16) for ic in range(2)]
                       for p in range(3)]
                PTbf = [sba(f"PTbf{j}", [128, WIN], BF16) for j in range(4)]
                negmax = [sba(f"negmax{p}", [128, 2], F32) for p in range(3)]
                rsum = [sba(f"rsum{p}", [128, 2], F32) for p in range(3)]
                rrec = [sba(f"rrec{p}", [128, 2], F32) for p in range(3)]
                Dbf = [sba(f"Dbf{p}", [128, 512], BF16) for p in range(3)]
                aout = [sba(f"aout{k}", [128, WIN], BF16) for k in range(NKT)]
                xbnd = sba("xbnd", [128, 8], F32)
                if s == 0 and l == 0:
                    print("SBUF watermark (attn):", nc.sbuf_base)

                t_dz = [kb.op("vector", lambda e, p=p: e.memset(Dbf[p][:, :], 0.0),
                              inc=sv) for p in range(3)]
                twq_g = []
                for g in range(4):
                    twq_g.append(kb.op("gpsimd", lambda e, g=g: e.dma_start(
                        wqkv.ap().rearrange("p (kc o) -> p kc o", kc=NKT)[:, 2 * g:2 * g + 2, :],
                        d_wqkv[l].rearrange("(kc p) o -> p kc o", p=128)[:, 2 * g:2 * g + 2, :]),
                        inc=wqg[g], by=16))
                twq_c = [twq_g[ci // 2] for ci in range(NKT)]
                twq = twq_g[-1]
                two = kb.op("gpsimd", lambda e: e.dma_start(
                    wo.ap().rearrange("p (kc o) -> p kc o", kc=NKT),
                    d_wo[l].rearrange("(kc p) o -> p kc o", p=128)),
                    inc=sd_wo2, by=16)

                def qk_stage(idx, side, kt, wb, lastmm, free_tok, sh_toks, twt_tok):
                    p = idx % 2
                    ti = (0, 1) if side == "q" else (2, 3)
                    qscol = lpc(l, 68) if side == "q" else lpc(l, 69)
                    bank = pb[p]
                    psA = pb[4 + 2 * p]
                    psB = pb[5 + 2 * p]
                    t_e = kb.op("scalar", lambda e, bank=bank, qscol=qscol, p=p: e.activation(
                        out=qbfe[p][:, :], in_=bank[:, 0:WIN], func=AF.Copy,
                        scale=qscol), waits=[lastmm, free_tok], inc=sa)
                    t_ef = kb.op("scalar", lambda e, bank=bank, qscol=qscol, p=p: e.activation(
                        out=qf[p], in_=bank[:, 0:WIN], func=AF.Copy,
                        scale=qscol), waits=[lastmm, free_tok], inc=sa)
                    t_s = kb.op("scalar", lambda e, bank=bank, p=p: e.activation(
                        out=q2b[p][:, :], in_=bank[:, 0:WIN], func=AF.Square),
                        waits=[lastmm, free_tok], inc=sa)
                    t_ss = kb.op("tensor", lambda e, p=p, psA=psA: e.matmul(
                        psA[0:2, 0:WIN], eq2[:, :], q2b[p][:, :],
                        start=True, stop=True), waits=[t_s, free_tok, CT], inc=sp)
                    t_l = kb.op("scalar", lambda e, p=p, psA=psA: e.activation(
                        out=ssr[p][:, :], in_=psA[0:2, 0:WIN], func=AF.Ln,
                        bias=eps24[0:2, 0:1]),
                        waits=[t_ss, free_tok, teps24], inc=sa)
                    t_x = kb.op("scalar", lambda e, p=p: e.activation(
                        out=ssr[p][:, :], in_=ssr[p][:, :], func=AF.Exp, scale=-0.5),
                        waits=[t_l], inc=sa)
                    t_bb = kb.op("tensor", lambda e, p=p, psB=psB: e.matmul(
                        psB[:, 0:WIN], e2f[:, :], ssr[p][:, :],
                        start=True, stop=True), waits=[t_x, free_tok, CT], inc=sp)
                    t_R = kb.op("tensor", lambda e, p=p, psA=psA: e.matmul(
                        psA[:, 0:WIN], perm[:, :], qbfe[p][:, :],
                        start=True, stop=True), waits=[t_e, t_l], inc=sp)
                    t_A = kb.op("vector", lambda e, p=p, i0=ti[0]: e.tensor_mul(
                        rotA[p], qf[p], twt[i0][:, :]),
                        waits=[t_ef, free_tok, twt_tok], inc=sv)
                    t_B = kb.op("vector", lambda e, p=p, i1=ti[1], psA=psA: e.tensor_mul(
                        rotB[p], psA[:, 0:WIN], twt[i1][:, :]),
                        waits=[t_R, twt_tok], inc=sv)
                    t_C = kb.op("vector", lambda e, p=p: e.tensor_add(
                        rotA[p], rotA[p], rotB[p]),
                        waits=[t_A, t_B], inc=sv)
                    if side == "q":
                        dst = q_rot[kt][:, :]
                    else:
                        dst = k_buf[kt][:, WIN:2 * WIN]
                    return kb.op("vector", lambda e, dst=dst, p=p, psB=psB: e.tensor_mul(
                        dst, rotA[p], psB[:, 0:WIN]),
                        waits=[t_C, t_bb] + sh_toks, inc=sv)

                def head_A(w, h, qk_all, afree):
                    hp, par = h // 2, h % 2
                    p = h % 3
                    h0 = par * 64
                    sim_bank = [pb[0], pb[1], pb[4]][p]
                    if w == 0:
                        live = [(0, 256, 128), (1, 256, 256)]
                        masks = [mskw0a, mskw0b]
                    else:
                        live = [(0, 0, 384), (1, 128, 384)]
                        masks = [mskg, mskg]
                    exp_toks = []
                    for (ic, jlo, width), msk in zip(live, masks):
                        t_sim = kb.op("tensor", lambda e, ic=ic, jlo=jlo, width=width, h0=h0, hp=hp, sim_bank=sim_bank: e.matmul(
                            sim_bank[:, 0:width],
                            q_rot[hp][h0:h0 + 64, ic * 128:(ic + 1) * 128],
                            k_buf[hp][h0:h0 + 64, jlo:jlo + width],
                            start=True, stop=True),
                            waits=qk_all + exp_toks + [afree], inc=sp)
                        t_msk = kb.op("vector", lambda e, ic=ic, width=width, msk=msk, sim_bank=sim_bank, p=p: e.tensor_tensor(
                            out=msk_sc[p][ic][:, 0:width], in0=sim_bank[:, 0:width],
                            in1=msk[:, 0:width], op=ALU.add),
                            waits=[t_sim, afree], inc=sv)
                        t_mx = kb.op("vector", lambda e, ic=ic, width=width, p=p: e.reduce_max(
                            out=negmax[p][:, ic:ic + 1], in_=msk_sc[p][ic][:, 0:width],
                            axis=AX.X, negate=True), waits=[t_msk], inc=sv)
                        t_ex = kb.op("scalar", lambda e, ic=ic, width=width, p=p: e.activation(
                            out=Pbf[p][ic][:, 0:width], in_=msk_sc[p][ic][:, 0:width],
                            func=AF.Exp, bias=negmax[p][:, ic:ic + 1],
                            accum_out=rsum[p][:, ic:ic + 1]),
                            waits=[t_mx, afree], inc=sa)
                        exp_toks.append(t_ex)
                    t_rc = kb.op("vector", lambda e, p=p: e.reciprocal(
                        rrec[p][:, 0:2], rsum[p][:, 0:2]), waits=exp_toks, inc=sv)
                    d_toks = []
                    for ic, dlo in ((0, 0), (1, 384)):
                        d_toks.append(kb.op("vector", lambda e, ic=ic, dlo=dlo, p=p: e.tensor_scalar_mul(
                            Dbf[p][:, dlo:dlo + 128], identb[:, :],
                            rrec[p][:, ic:ic + 1]),
                            waits=[t_rc, afree] + t_dz, inc=sv))
                    return exp_toks, d_toks

                def head_B(w, h, stA, prev_ptev, outfree, pvfree):
                    hp, par = h // 2, h % 2
                    p = h % 3
                    h0 = par * 64
                    ovb = pb[5] if par == 0 else pb[7]
                    exp_toks, d_toks = stA
                    if w == 0:
                        pt_plan = {2: [(0, 0)], 3: [(1, 0)]}
                        pt_plan = {2: [(0, 0), (1, 0)], 3: [(1, 128)]}
                        pv_plan = [2, 3]
                    else:
                        pt_plan = {0: [(0, 0)], 1: [(0, 128), (1, 0)],
                                   2: [(0, 256), (1, 128)], 3: [(1, 256)]}
                        pv_plan = [0, 1, 2, 3]
                    pt_evac = {}
                    lastpt = None
                    for jc, contribs in pt_plan.items():
                        bank = pb[2 + (jc % 2)]
                        off = (jc // 2) * 256
                        if jc - 2 in pt_evac:
                            bfree = pt_evac[jc - 2]
                        elif prev_ptev:
                            bfree = prev_ptev.get(jc + 2) or prev_ptev.get(jc)
                        else:
                            bfree = None
                        for n_, (ic, ploc) in enumerate(contribs):
                            lastpt = kb.op("tensor", lambda e, ic=ic, ploc=ploc, off=off, n_=n_, bank=bank, p=p, contribs=contribs: e.matmul(
                                bank[:, off: off + 256],
                                Pbf[p][ic][:, ploc:ploc + 128],
                                Dbf[p][:, ic * 256:(ic + 1) * 256],
                                start=(n_ == 0), stop=(n_ == len(contribs) - 1)),
                                waits=exp_toks + d_toks + [lastpt, bfree], inc=sp)
                        if jc % 2 == 0:
                            pt_evac[jc] = kb.op("scalar", lambda e, jc=jc, off=off, bank=bank: e.copy(
                                PTbf[jc][:, :], bank[:, off: off + 256]),
                                waits=[lastpt, pvfree], inc=sa)
                        else:
                            pt_evac[jc] = kb.op("vector", lambda e, jc=jc, off=off, bank=bank: e.tensor_copy(
                                PTbf[jc][:, :], bank[:, off: off + 256]),
                                waits=[lastpt, pvfree], inc=sv)
                    lastpv = None
                    for n_, jc in enumerate(pv_plan):
                        lastpv = kb.op("tensor", lambda e, jc=jc, h0=h0, hp=hp, par=par, n_=n_, ovb=ovb: e.matmul(
                            ovb[h0:h0 + 64, 0:WIN],
                            v_buf[jc][:, (2 * hp + par) * 64: (2 * hp + par) * 64 + 64],
                            PTbf[jc][:, :],
                            start=(n_ == 0), stop=(n_ == len(pv_plan) - 1),
                            tile_position=(0, h0)),
                            waits=(list(pt_evac.values()) + [outfree, lastpv]) if n_ == 0 else [("soft", lastpv)],
                            inc=(sp if n_ == len(pv_plan) - 1 else None))
                    t_ev = kb.op("scalar", lambda e, hp=hp, h0=h0, ovb=ovb: e.copy(
                        aout[hp][h0:h0 + 64, :], ovb[h0:h0 + 64, :WIN]),
                        waits=[lastpv], inc=sa)
                    return max(pt_evac), lastpt, pt_evac, t_ev, lastpv

                shift_src = None
                wo_done = None
                for w in range(NW):
                    wb = w * WIN
                    sh_toks = []
                    if w > 0:
                        for k in range(NKT):
                            sh_toks.append(kb.op("vector", lambda e, k=k: e.tensor_copy(
                                k_buf[k][:, 0:WIN], k_buf[k][:, WIN:2 * WIN]),
                                waits=[shift_src], inc=sv))
                        for j in range(2):
                            sh_toks.append(kb.op("vector", lambda e, j=j: e.tensor_copy(
                                v_buf[j][:, :], v_buf[2 + j][:, :]),
                                waits=[shift_src], inc=sv))
                    xb_toks = []
                    for k in range(NKT):
                        xb_toks.append(kb.op("scalar", lambda e, k=k, wb=wb: e.copy(
                            xbfw[k][:, :], xT[k][:, wb:wb + WIN + 1]),
                            waits=[wo_done], inc=sa))
                    if w > 0:
                        for k in range(4, NKT):
                            xb_toks.append(kb.op("scalar", lambda e, k=k: e.copy(
                                xbfw[k][:, 0:1], xbnd[:, k:k + 1]),
                                waits=[xb_toks[k], snap_toks[k - 4]], inc=sa))
                    twt_tok = None
                    for i in range(4):
                        tsrc, tbase = tabsrc[i]
                        for hh in range(2):
                            twt_tok = kb.op("vector", lambda e, i=i, hh=hh, wb=wb, tsrc=tsrc, tbase=tbase: e.tensor_copy(
                                twt[i][hh * 64:(hh + 1) * 64, :],
                                tsrc[tbase:tbase + 64, wb:wb + WIN]),
                                waits=[CT, wo_done], inc=sv)
                    insts = [(side, kt, obase) for side, obase in
                             (("q", 0), ("k", DIM)) for kt in range(NKT)]
                    d_tok = {}
                    qk_toks = []
                    for idx, (side, kt, obase) in enumerate(insts):
                        p = idx % 2
                        bank = pb[p]
                        free_tok = d_tok.get(idx - 2 if pipe[1] else idx - 1)
                        lastmm = None
                        for ci in range(NKT):
                            lastmm = kb.op("tensor", lambda e, kt=kt, ci=ci, obase=obase, bank=bank: e.matmul(
                                bank[:, 0:WIN],
                                wqkv[:, ci * 3 * DIM + obase + kt * 128: ci * 3 * DIM + obase + (kt + 1) * 128],
                                xbfw[ci][:, (1 if ci < 4 else 0): (1 if ci < 4 else 0) + WIN],
                                start=(ci == 0), stop=(ci == NKT - 1)),
                                waits=(xb_toks + [twq_c[ci], free_tok]) if ci == 0 else [twq_c[ci], ("soft", lastmm)],
                                inc=(sp if ci == NKT - 1 else None))
                        d_tok[idx] = qk_stage(idx, side, kt, wb, lastmm, free_tok,
                                              sh_toks, twt_tok)
                        qk_toks.append(d_tok[idx])
                    v_toks = []
                    for th in range(2):
                        lastmm = [None, None]
                        for ci in range(NKT):
                            for eh in range(2):
                                lastmm[eh] = kb.op("tensor", lambda e, th=th, ci=ci, eh=eh: e.matmul(
                                    pb[2 + eh][:, :],
                                    xbfw[ci][:, (1 if ci < 4 else 0) + th * 128: (1 if ci < 4 else 0) + (th + 1) * 128],
                                    wqkv[:, ci * 3 * DIM + 2 * DIM + eh * 512: ci * 3 * DIM + 2 * DIM + (eh + 1) * 512],
                                    start=(ci == 0), stop=(ci == NKT - 1)),
                                    waits=(xb_toks + [twq_c[ci]] + sh_toks + v_toks) if ci == 0 else [twq_c[ci], ("soft", lastmm[eh])],
                                    inc=(sp if ci == NKT - 1 else None))
                        for eh in range(2):
                            v_toks.append(kb.op("scalar", lambda e, th=th, eh=eh: e.copy(
                                v_buf[2 + th][:, eh * 512:(eh + 1) * 512],
                                pb[2 + eh][:, :]), waits=[lastmm[eh]], inc=sa))
                    qk_all = qk_toks + v_toks + sh_toks
                    if WDBG is not None and s == 0 and l == 0 and w == WDBG:
                        for k in range(NKT):
                            kb.op("gpsimd", lambda e, k=k: e.dma_start(
                                d_dbg["qrot"][k], q_rot[k][:, :]), waits=qk_all,
                                inc=st_, by=16)
                            kb.op("gpsimd", lambda e, k=k: e.dma_start(
                                d_dbg["krot"][k], k_buf[k][:, :]), waits=qk_all,
                                inc=st_, by=16)
                        for j in range(4):
                            kb.op("gpsimd", lambda e, j=j: e.dma_start(
                                d_dbg["vtok"][j], v_buf[j][:, :]), waits=qk_all,
                                inc=st_, by=16)
                    stA = {}
                    Bres = {}
                    ao_evacs = []
                    if pipe[0]:
                        stA[0] = head_A(w, 0, qk_all, None)
                        stA[1] = head_A(w, 1, qk_all, None)
                        stA[2] = head_A(w, 2, qk_all, None)
                        for h in range(16):
                            pt_prev = Bres.get(h - 1)
                            prev_ptev = pt_prev[2] if pt_prev else None
                            out_prev = Bres.get(h - 2)
                            outfree = out_prev[3] if out_prev else None
                            pv_prev = Bres.get(h - 1)
                            pvfree = pv_prev[4] if pv_prev else None
                            Bres[h] = head_B(w, h, stA[h], prev_ptev, outfree,
                                             pvfree)
                            ao_evacs.append(Bres[h][3])
                            if h + 3 < 16:
                                stA[h + 3] = head_A(w, h + 3, qk_all, Bres[h][1])
                    else:
                        prev_ev = None
                        for h in range(16):
                            stA[h] = head_A(w, h, qk_all, prev_ev)
                            pt_prev = Bres.get(h - 1)
                            prev_ptev = pt_prev[2] if pt_prev else None
                            pvfree = pt_prev[4] if pt_prev else None
                            Bres[h] = head_B(w, h, stA[h], prev_ptev, prev_ev,
                                             pvfree)
                            prev_ev = Bres[h][3]
                            ao_evacs.append(prev_ev)
                    if WDBG is not None and s == 0 and l == 0 and w == WDBG:
                        for k in range(NKT):
                            kb.op("gpsimd", lambda e, k=k: e.dma_start(
                                d_dbg["aout"][k], aout[k][:, :]), waits=ao_evacs,
                                inc=st_, by=16)
                    snap_toks = []
                    for k in range(4, NKT):
                        snap_toks.append(kb.op("vector", lambda e, k=k, wb=wb: e.tensor_copy(
                            xbnd[:, k:k + 1], xT[k][:, wb + WIN:wb + WIN + 1]),
                            waits=[ao_evacs[-1]], inc=sv))
                    wo_toks = []
                    for kt in range(NKT):
                        bank = pb[6] if kt % 2 == 0 else pb[5]
                        lastmm = None
                        for ci in range(NKT):
                            lastmm = kb.op("tensor", lambda e, kt=kt, ci=ci, bank=bank: e.matmul(
                                bank[:, 0:WIN],
                                wo[:, ci * DIM + kt * 128: ci * DIM + (kt + 1) * 128],
                                aout[ci][:, :],
                                start=(ci == 0), stop=(ci == NKT - 1)),
                                waits=(ao_evacs + [two] + wo_toks[-2:]) if ci == 0 else [("soft", lastmm)],
                                inc=(sp if ci == NKT - 1 else None))
                        wo_toks.append(kb.op("vector", lambda e, kt=kt, wb=wb, bank=bank: e.tensor_add(
                            xT[kt][:, 1 + wb: 1 + wb + WIN],
                            bank[:, 0:WIN], xT[kt][:, 1 + wb: 1 + wb + WIN]),
                            waits=[lastmm] + snap_toks, inc=sv))
                    wo_done = wo_toks[-1]
                    shift_src = wo_done
                if f"L{l}a" in dbg:
                    dump_x(s, f"L{l}a", wo_done)

        def run_ffn(s, l):
            kb.barrier()
            with ExitStack() as pf:
                sbf = lambda n_, sh, d_: pf.enter_context(nc.sbuf_tensor(_nm(n_), sh, d_))
                w2 = sbf("w2", [128, NPAIR * DIM], BF16)
                w1p = [sbf(f"w1p{i}", [128, NKT * 256], BF16) for i in range(3)]
                xnbf = [sbf(f"xnbf{k}", [128, 512], BF16) for k in range(NKT)]
                h2 = [sbf(f"h2{i}", [128, NPAIR * 512], BF16) for i in range(2)]
                gel = [sbf(f"gel{p}", [128, 512], BF16) for p in range(2)]
                rtmp = sbf("rtmp", [1, 512], F32)
                rnegmu = sbf("rnegmu", [1, 512], F32)
                rrs = rnegmu
                xbf_s = sbf("xbf_s", [128, 512], BF16)
                xsq_s = sbf("xsq_s", [128, 512], BF16)
                t0f = [sbf(f"t0f{p}", [128, 512], F32) for p in range(2)]
                if s == 0 and l == 0:
                    print("SBUF watermark (ffn):", nc.sbuf_base)

                groups = [(0, 6), (6, 12), (12, 18), (18, 22)]
                tw2_c = [None] * NPAIR
                for gi, (a_, b_) in enumerate(groups):
                    if s == 0:
                        tok = kb.op("gpsimd", lambda e, a_=a_, b_=b_: e.dma_start(
                            w2[:, a_ * DIM:b_ * DIM], d_w2[l, :, a_ * DIM:b_ * DIM]),
                            inc=w2g[gi], by=16)
                        wb1_tok[("w2", l, gi)] = kb.op(
                            "sync", lambda e, a_=a_, b_=b_: e.dma_start(
                                d_w2b[l, :, a_ * DIM:b_ * DIM],
                                w2[:, a_ * DIM:b_ * DIM]),
                            waits=[tok], inc=swb, by=16)
                    else:
                        tok = kb.op("sync", lambda e, a_=a_, b_=b_: e.dma_start(
                            w2[:, a_ * DIM:b_ * DIM], d_w2b[l, :, a_ * DIM:b_ * DIM]),
                            waits=[wb1_tok[("w2", l, gi)]], inc=w2gs[gi], by=16)
                    for kchunk in range(a_, b_):
                        tw2_c[kchunk] = tok
                tw2 = tw2_c[21]
                w1_free = [[], [], []]
                ENGS = ("sync", "scalar", "gpsimd")

                def load_pair(j, first):
                    par = j % 3
                    if first:
                        tok = kb.op("gpsimd", lambda e, par=par, j=j: e.dma_start(
                            w1p[par][:, :], d_w1[l, j]),
                            waits=w1_free[par], inc=w1s[par % 2], by=16)
                        wb1_tok[(l, j)] = kb.op(
                            "sync", lambda e, par=par, j=j: e.dma_start(
                                d_w1b[l, j], w1p[par][:, :]),
                            waits=[tok], inc=swb, by=16)
                        return tok
                    eng = ENGS[par]
                    return kb.op(eng, lambda e, par=par, j=j: e.dma_start(
                        w1p[par][:, :], d_w1b[l, j]),
                        waits=w1_free[par] + [wb1_tok[(l, j)]],
                        inc=w1se[eng], by=16)

                rows_free = None
                bcast_free = [None, None]
                stats_free = None
                conv_free = None
                xn_frees = [None]

                def stats_apply(tc):
                    nonlocal rows_free, bcast_free, stats_free, conv_free
                    stats_last = None
                    sq_last = conv_free
                    for k in range(NKT):
                        off = (1 if k < 4 else 0) + tc * 512
                        t_c1 = kb.op("scalar", lambda e, k=k, off=off: e.copy(
                            xbf_s[:, :], xT[k][:, off:off + 512]),
                            waits=[sq_last], inc=sa)
                        t_c2 = kb.op("scalar", lambda e, k=k, off=off: e.activation(
                            out=xsq_s[:, :], in_=xT[k][:, off:off + 512],
                            func=AF.Square), waits=[t_c1], inc=sa)
                        t_m1 = kb.op("tensor", lambda e, k=k: e.matmul(
                            pb[2][0:33, :], st1[:, :], xbf_s[:, :],
                            start=(k == 0), stop=False, skip_group_check=True),
                            waits=[t_c1, ("soft", stats_last)] +
                                  ([stats_free] if k == 0 else []), inc=sp)
                        t_m2 = kb.op("tensor", lambda e, k=k: e.matmul(
                            pb[2][0:33, :], st33[:, :], xsq_s[:, :],
                            start=False, stop=(k == NKT - 1), skip_group_check=True),
                            waits=[t_c2, ("soft", t_m1)], inc=sp)
                        stats_last = t_m2
                        sq_last = t_m2
                    conv_free = stats_last
                    t1 = kb.op("vector", lambda e: e.tensor_scalar_mul(
                        rnegmu[0:1, :], pb[2][0:1, :], -1.0),
                        waits=[stats_last, rows_free], inc=sv)
                    t6 = kb.op("tensor", lambda e: e.matmul(
                        pb[0][:, :], onesrow[0:1, :], rnegmu[0:1, :],
                        start=True, stop=True),
                        waits=[t1, bcast_free[0], CT], inc=sp)
                    t2 = kb.op("vector", lambda e: e.tensor_mul(
                        rtmp[0:1, :], rnegmu[0:1, :], rnegmu[0:1, :]),
                        waits=[t1], inc=sv)
                    t3 = kb.op("vector", lambda e: e.tensor_sub(
                        rtmp[0:1, :], pb[2][32:33, :], rtmp[0:1, :]),
                        waits=[t2], inc=sv)
                    t4 = kb.op("scalar", lambda e: e.activation(
                        out=rrs[0:1, :], in_=rtmp[0:1, :], func=AF.Ln,
                        bias=epscol[0:1, 0:1]), waits=[t3, teps, t6], inc=sa)
                    t5 = kb.op("scalar", lambda e: e.activation(
                        out=rrs[0:1, :], in_=rrs[0:1, :], func=AF.Exp, scale=-0.5),
                        waits=[t4], inc=sa)
                    t7 = kb.op("tensor", lambda e: e.matmul(
                        pb[1][:, :], onesrow[0:1, :], rrs[0:1, :],
                        start=True, stop=True),
                        waits=[t5, bcast_free[1]], inc=sp)
                    rows_free = t7
                    xn_toks = []
                    ap_last = {}
                    for k in range(NKT):
                        off = (1 if k < 4 else 0) + tc * 512
                        p = k % 2
                        ta = kb.op("vector", lambda e, k=k, off=off, p=p: e.tensor_add(
                            t0f[p][:, :], xT[k][:, off:off + 512], pb[0][:, :]),
                            waits=[t6, ap_last.get(k - 2)] + xn_frees, inc=sv)
                        tb_ = kb.op("vector", lambda e, p=p: e.tensor_mul(
                            t0f[p][:, :], t0f[p][:, :], pb[1][:, :]),
                            waits=[ta, t7], inc=sv)
                        tcx = kb.op("vector", lambda e, k=k, l=l, p=p: e.tensor_scalar(
                            out=xnbf[k][:, :], in0=t0f[p][:, :], scalar1=lpc(l, k),
                            scalar2=lpc(l, 8 + k), op0=ALU.mult, op1=ALU.add),
                            waits=[tb_], inc=sv)
                        xn_toks.append(tcx)
                        ap_last[k] = tcx
                    return xn_toks

                tcs = list(reversed(range(NT512)))
                xn_toks = stats_apply(tcs[0])
                evac_hist = []
                h2_guard = [None, None]
                for i, tc in enumerate(tcs):
                    hb = h2[i % 2]
                    first = (s == 0 and i == 0)
                    w1_toks = {jj: load_pair(jj, first) for jj in range(3)}
                    h2_tok = {}
                    for j in range(NPAIR):
                        par = j % 3
                        lmm = [None, None]
                        for half in range(2):
                            bank = pb[2 * (j % 2) + half]
                            for ci in range(NKT):
                                lmm[half] = kb.op("tensor", lambda e, j=j, par=par, half=half, ci=ci, bank=bank: e.matmul(
                                    bank[:, :],
                                    w1p[par][:, ci * 256 + half * 128: ci * 256 + half * 128 + 128],
                                    xnbf[ci][:, :],
                                    start=(ci == 0), stop=(ci == NKT - 1)),
                                    waits=([w1_toks[j], h2_tok.get(j - 2 if pipe[2] else j - 1), xn_toks[0]] +
                                           ([xn_toks[NKT - 1]] if j == 0 else [])) if ci == 0
                                    else [("soft", lmm[half]), xn_toks[ci]],
                                    inc=(sp if ci == NKT - 1 else None))
                        w1_free[par] = [lmm[1]] + (
                            [wb1_tok[(l, j)]] if first else [])
                        if j + 3 < NPAIR:
                            w1_toks[j + 3] = load_pair(j + 3, first)
                        tg_ = kb.op("scalar", lambda e, j=j, l=l: e.activation(
                            out=gel[j % 2][:, :], in_=pb[2 * (j % 2) + 1][:, :],
                            func=AF.Gelu, bias=lpc(l, 17 + 2 * j)),
                            waits=[lmm[1], h2_tok.get(j - 2 if pipe[2] else j - 1)], inc=sa)
                        h2_tok[j] = kb.op("vector", lambda e, j=j, l=l, hb=hb: e.scalar_tensor_tensor(
                            out=hb[:, j * 512:(j + 1) * 512], in0=pb[2 * (j % 2)][:, :],
                            scalar=lpc(l, 16 + 2 * j), in1=gel[j % 2][:, :],
                            op0=ALU.add, op1=ALU.mult),
                            waits=[lmm[0], tg_, h2_guard[i % 2]], inc=sv)
                    w1_done = lmm[1]
                    out_toks = []
                    for kt in range(NKT):
                        bank = pb[4 + kt % 4]
                        lastmm = None
                        for j in range(NPAIR):
                            lastmm = kb.op("tensor", lambda e, j=j, kt=kt, bank=bank, hb=hb: e.matmul(
                                bank[:, :],
                                w2[:, j * DIM + kt * 128: j * DIM + (kt + 1) * 128],
                                hb[:, j * 512:(j + 1) * 512],
                                start=(j == 0), stop=(j == NPAIR - 1)),
                                waits=([h2_tok[j], tw2_c[j]] + evac_hist[-4:]) if j == 0 else [h2_tok[j], tw2_c[j], ("soft", lastmm)],
                                inc=(sp if j == NPAIR - 1 else None))
                        out_toks.append(kb.op("vector", lambda e, kt=kt, tc=tc, l=l, bank=bank: e.scalar_tensor_tensor(
                            out=xT[kt][:, 1 + tc * 512: 1 + (tc + 1) * 512],
                            in0=bank[:, :], scalar=lpc(l, 60 + kt),
                            in1=xT[kt][:, 1 + tc * 512: 1 + (tc + 1) * 512],
                            op0=ALU.add, op1=ALU.add), waits=[lastmm], inc=sv))
                        evac_hist.append(out_toks[-1])
                    h2_guard[i % 2] = out_toks[-1]
                    if i + 1 < len(tcs):
                        stats_free = h2_tok[NPAIR - 1]
                        bcast_free = [h2_tok[NPAIR - 2], h2_tok[NPAIR - 2]]
                        xn_frees = [h2_tok[NPAIR - 1], h2_tok[NPAIR - 2]]
                        xn_toks = stats_apply(tcs[i + 1])
                    prev_out = out_toks[-1]
                if f"L{l}f" in dbg:
                    dump_x(s, f"L{l}f", prev_out)

        def run_final(s):
            kb.barrier()
            with ExitStack() as p3:
                sb3 = lambda n_, sh, d_: p3.enter_context(nc.sbuf_tensor(_nm(n_), sh, d_))
                gtok = sb3("gtok", [128, DIM], F32)
                stok = [sb3(f"stok{i}", [128, DIM], F32) for i in range(2)]
                x0tok = [sb3(f"x0tok{i}", [128, DIM], F32) for i in range(2)]
                otok = [sb3(f"otok{i}", [128, DIM], F32) for i in range(2)]
                prev = [None, None]
                prev_g = None
                prev_p3 = {}
                for tc in range(NT128):
                    par = tc % 2
                    kb.op("sync", lambda e, tc=tc, par=par: e.dma_start(
                        stok[par][:, :], d_sscr[s, tc]), waits=[prev[par]],
                        inc=sd_par[par], by=16)
                    tlx = kb.op("sync", lambda e, tc=tc, par=par: e.dma_start(
                        x0tok[par][:, :], d_x[s, tc * 128:(tc + 1) * 128, :]),
                        waits=[prev[par]], inc=sd_par[par], by=16)
                    tls = tlx
                    gevs = {}
                    for k in range(NKT):
                        bank = pb[4 + (k % 2)]
                        tp = kb.op("tensor", lambda e, k=k, tc=tc, bank=bank: e.transpose(
                            bank[:, 0:128],
                            xT[k][:, 1 + tc * 128: 1 + (tc + 1) * 128],
                            identf[:, :]),
                            waits=[gevs.get(k - 2), prev_g, prev_p3.get(k % 2)],
                            inc=sp)
                        gevs[k] = kb.op("scalar", lambda e, k=k, bank=bank: e.activation(
                            out=gtok[:, k * 128:(k + 1) * 128], in_=bank[:, 0:128],
                            func=AF.Silu), waits=[tp], inc=sa)
                    prev_p3[0] = gevs[NKT - 2]
                    prev_p3[1] = gevs[NKT - 1]
                    t1 = kb.op("vector", lambda e, par=par: e.tensor_mul(
                        otok[par][:, :], stok[par][:, :], gtok[:, :]),
                        waits=[gevs[NKT - 1], tls], inc=sv)
                    prev_g = t1
                    t2 = kb.op("vector", lambda e, par=par: e.tensor_add(
                        otok[par][:, :], otok[par][:, :], x0tok[par][:, :]),
                        waits=[t1, tlx], inc=sv)
                    prev[par] = kb.op("sync", lambda e, tc=tc, par=par: e.dma_start(
                        d_out[s, tc * 128:(tc + 1) * 128, :], otok[par][:, :]),
                        waits=[t2], inc=st_par[par], by=16)

        def dump_x(s, tag, wait_tok):
            if s != 0:
                return
            for k in range(NKT):
                kb.op("sync", lambda e, k=k, tag=tag: e.dma_start(
                    d_dbg[tag][k], xT[k][:, 1:T + 1]), waits=[wait_tok],
                    inc=st_, by=16)

        for s in range(NSEQ):
            run_p0(s)
            run_sbranch(s)
            for l in range(DEPTH):
                run_attn(s, l)
                run_ffn(s, l)
            if "xt" in dbg:
                kb.barrier()
                dump_x(s, "xt", None)
            run_final(s)

        kb.op("sync", None, waits=[(st_, st_.n), (st_par[0], st_par[0].n),
                                   (st_par[1], st_par[1].n)])
        kb.emit()
    return nc


def pack_inputs(ins, T, NSEQ, DEPTH, consts):
    m = {}
    m["decay_col"] = np.full((128, 1), np.float32(ins["decay"]), np.float32)
    m["wT"] = np.ascontiguousarray(ins["W"].T).astype(np.float32)
    m["grep"] = np.broadcast_to(ins["gamma"], (128, DIM)).astype(np.float32).copy()
    m["brep"] = np.broadcast_to(ins["beta"], (128, DIM)).astype(np.float32).copy()
    m["yrep"] = np.broadcast_to(ins["b"], (128, DIM)).astype(np.float32).copy()
    m["wqkv"] = np.ascontiguousarray(ins["Wqkv"]).astype(np.float32)
    m["wo"] = np.ascontiguousarray(ins["Wo"]).astype(np.float32)
    w1 = np.asarray(ins["W1"], np.float32)
    w1pk = np.zeros((DEPTH, NPAIR, 128, NKT * 256), np.float32)
    for l in range(DEPTH):
        resh = w1[l].reshape(NKT, 128, 2 * DI)
        for j in range(NPAIR):
            ncol = min(128, DI - j * 128)
            t = w1pk[l, j].reshape(128, NKT, 256)
            t[:, :, 0:ncol] = np.transpose(
                resh[:, :, j * 128: j * 128 + ncol], (1, 0, 2))
            t[:, :, 128:128 + ncol] = np.transpose(
                resh[:, :, DI + j * 128: DI + j * 128 + ncol], (1, 0, 2))
    m["w1"] = w1pk
    w2 = np.asarray(ins["W2"], np.float32)
    w2pk = np.zeros((DEPTH, 128, NPAIR * DIM), np.float32)
    for l in range(DEPTH):
        for kc in range(21):
            w2pk[l, :, kc * DIM:(kc + 1) * DIM] = w2[l][kc * 128:(kc + 1) * 128, :]
        w2pk[l, 0:DI - 21 * 128, 21 * DIM:] = w2[l][21 * 128:, :]
    m["w2"] = w2pk
    lpv = np.zeros((DEPTH, 128, 70), np.float32)
    for l in range(DEPTH):
        lpv[l, :, 0:8] = ins["ln_g"][l].reshape(8, 128).T
        lpv[l, :, 8:16] = ins["ln_b"][l].reshape(8, 128).T
        b1 = ins["b1"][l]
        for j in range(NPAIR):
            n = min(128, DI - j * 128)
            lpv[l, 0:n, 16 + 2 * j] = b1[j * 128: j * 128 + n]
            lpv[l, 0:n, 17 + 2 * j] = b1[DI + j * 128: DI + j * 128 + n]
        lpv[l, :, 60:68] = ins["b2"][l].reshape(8, 128).T
        lpv[l, :, 68] = np.concatenate([ins["q_scale"][l]] * 2)
        lpv[l, :, 69] = np.concatenate([ins["k_scale"][l]] * 2)
    m["lp"] = lpv
    for k, v in consts.items():
        m[k] = v
    return m


_CACHE = {}


def kernel(L0_attn_Z, L0_attn_decay, L0_attn_W, L0_attn_b, L0_attn_gamma,
           L0_attn_beta, L0_ffn_Wqkv, L0_ffn_q_scale, L0_ffn_k_scale,
           L0_ffn_Wo, L0_ffn_ln_g, L0_ffn_ln_b, L0_ffn_W1, L0_ffn_b1,
           L0_ffn_W2, L0_ffn_b2):
    x = np.asarray(L0_attn_Z)
    B, T, D = x.shape
    NSEQ = B // 8
    DEPTH = 6
    ins = dict(decay=np.float32(L0_attn_decay), W=np.asarray(L0_attn_W),
               b=np.asarray(L0_attn_b), gamma=np.asarray(L0_attn_gamma),
               beta=np.asarray(L0_attn_beta), Wqkv=np.asarray(L0_ffn_Wqkv),
               q_scale=np.asarray(L0_ffn_q_scale),
               k_scale=np.asarray(L0_ffn_k_scale), Wo=np.asarray(L0_ffn_Wo),
               ln_g=np.asarray(L0_ffn_ln_g), ln_b=np.asarray(L0_ffn_ln_b),
               W1=np.asarray(L0_ffn_W1), b1=np.asarray(L0_ffn_b1),
               W2=np.asarray(L0_ffn_W2), b2=np.asarray(L0_ffn_b2))
    key = (T, NSEQ, DEPTH)
    if key not in _CACHE:
        _CACHE[key] = build_program(T=T, NSEQ=NSEQ, DEPTH=DEPTH)
    nc = _CACHE[key]
    consts = make_consts(T)
    common = pack_inputs(ins, T, NSEQ, DEPTH, consts)
    in_maps = []
    for c in range(8):
        im = dict(common)
        im["x"] = np.ascontiguousarray(x[c * NSEQ:(c + 1) * NSEQ])
        in_maps.append(im)
    res = run_bass_kernel_spmd(nc, in_maps, core_ids=list(range(8)))
    out = np.concatenate([r["out"] for r in res.results], axis=0)
    return out.astype(np.float32)
